# version 11
# baseline (speedup 1.0000x reference)
import numpy as np
import ml_dtypes
from contextlib import ExitStack

import concourse.bass as bass
import concourse.mybir as mybir
from concourse.bass_utils import run_bass_kernel_spmd
from concourse.alu_op_type import AluOpType

F32 = mybir.dt.float32
BF16 = mybir.dt.bfloat16
AF = mybir.ActivationFunctionType
AX = mybir.AxisListType
BF = ml_dtypes.bfloat16

NCORES = 8
S, D, H, HD = 8192, 2048, 16, 128
CH = 512
KC = 16
TOW = 1024
B_OFF = 30.0
NEG = -1e30
WRING = 4
QSC = float(HD) ** -0.5

SLOPES = 2.0 ** (-8.0 / np.arange(1, H + 1, dtype=np.float64))


def _hilo(v):
    hi = v.astype(BF)
    lo = (v.astype(np.float32) - hi.astype(np.float32)).astype(BF)
    return hi, lo


def _tile_w(wt, Gm=128):
    K, M = wt.shape
    Kc, G = K // 128, M // Gm
    t = wt.reshape(Kc, 128, G, Gm).transpose(2, 1, 0, 3).reshape(G, 128, Kc * Gm)
    return np.ascontiguousarray(t, dtype=np.float32)


def _host_rk4r(core):
    chunks = [core, 15 - core]
    out = np.zeros((H, 128, 32 * 512), dtype=BF)
    qt = np.arange(CH, dtype=np.float64)
    for h in range(H):
        sl = SLOPES[h]
        for s in range(2):
            cs = chunks[s]
            i_abs = CH * cs + qt
            for it in range(16):
                j = s * 16 + it
                blk, base = j, 0
                col = blk * 512
                if it == 15:
                    v = sl * (CH * cs - i_abs) - B_OFF
                elif it < cs and (s == 1 or it <= 6):
                    v = sl * (CH * it - i_abs) - B_OFF
                else:
                    v = np.full(CH, NEG)
                hi, lo = _hilo(v.astype(np.float32))
                out[h, base + 0, col:col + 512] = hi
                out[h, base + 1, col:col + 512] = lo
                out[h, base + 2, col:col + 512] = BF(1.0)
                out[h, base + 3, col:col + 512] = BF(1.0)
    return out


def _host_rk4l():
    out = np.zeros((128, H * 4 * 128), dtype=BF)
    kt = np.arange(128, dtype=np.float64)
    for h in range(H):
        for sub in range(4):
            col = (h * 4 + sub) * 128
            hi, lo = _hilo((SLOPES[h] * (128 * sub + kt)).astype(np.float32))
            for base in (0,):
                out[base + 0, col:col + 128] = BF(1.0)
                out[base + 1, col:col + 128] = BF(1.0)
                out[base + 2, col:col + 128] = hi
                out[base + 3, col:col + 128] = lo
    return out


def _host_tri():
    tri = np.zeros((128, 4 * 512), dtype=np.float32)
    ktc = np.arange(128)[:, None]
    qtc = np.arange(512)[None, :]
    for sub in range(4):
        tri[:, sub * 512:(sub + 1) * 512] = np.where(
            128 * sub + ktc <= qtc, 0.0, NEG)
    return tri


def _own(k):
    return (k, 0) if k <= 7 else (15 - k, 1)


class Prog:
    def __init__(self, nc, es):
        self.nc = nc
        self.items = {k: [] for k in ("tensor", "vector", "scalar", "sync", "gpsimd")}
        self.sem_n = {}
        self.handles = {}
        self.last_wait = {}
        self.es = es

    def sem(self, name):
        self.handles[name] = self.es.enter_context(self.nc.semaphore(name))
        self.sem_n[name] = 0

    def wait(self, eng, sem, val):
        if val <= 0:
            return
        assert val <= self.sem_n[sem], f"wait {sem}>={val} > {self.sem_n[sem]}"
        key = (eng, sem)
        if self.last_wait.get(key, -1) >= val:
            return
        self.last_wait[key] = val
        h = self.handles[sem]
        self.items[eng].append(lambda e, h=h, v=val: e.wait_ge(h, v))

    def emit(self, eng, fn, inc=None, inck=1):
        if inc is not None:
            h = self.handles[inc]
            self.sem_n[inc] += inck
            self.items[eng].append(lambda e, fn=fn, h=h, k=inck: fn(e).then_inc(h, k))
            return self.sem_n[inc]
        self.items[eng].append(fn)
        return None

    def dma(self, q, out, in_, inc, transpose=False, slow=False):
        return self.emit(
            q, lambda e, o=out, i=in_, t=transpose, sl=slow: e.dma_start(
                out=o, in_=i, transpose=t, allow_slow_non_contiguous=sl),
            inc=inc, inck=16)


def build_nc(dbg=0):
    nc = bass.Bass("TRN2", target_bir_lowering=False)
    dt = nc.dram_tensor

    xin = dt("xin", [TOW, D], F32, kind="ExternalInput").ap()
    cin = dt("cin", [D], F32, kind="ExternalInput").ap()
    wmods = dt("wmods", [3, 128, 8192], F32, kind="ExternalInput").ap()
    wq_d = dt("wq", [48, 128, 2048], F32, kind="ExternalInput").ap()
    wo_d = dt("wo", [16, 128, 2048], F32, kind="ExternalInput").ap()
    g9_d = dt("g9", [128, 144], F32, kind="ExternalInput").ap()
    sgu_d = dt("sgu", [88, 128, 2048], F32, kind="ExternalInput").ap()
    sdn_d = dt("sdn", [4, 16, 128, 1408], F32, kind="ExternalInput").ap()
    w13_d = dt("w13", [8, 22, 128, 2048], F32, kind="ExternalInput").ap()
    w2_d = dt("w2", [8, 16, 128, 1408], F32, kind="ExternalInput").ap()
    rk4l_d = dt("rk4l", [128, H * 4 * 128], BF16, kind="ExternalInput").ap()
    rk4r_d = dt("rk4r", [H, 128, 32 * 512], BF16, kind="ExternalInput").ap()
    tri_d = dt("tri", [128, 4 * 512], F32, kind="ExternalInput").ap()
    iden16_d = dt("iden16", [128, 128], BF16, kind="ExternalInput").ap()
    iden32_d = dt("iden32", [128, 128], F32, kind="ExternalInput").ap()
    ones128_d = dt("ones128", [128, 1], BF16, kind="ExternalInput").ap()
    e0mat_d = dt("e0mat", [128, 128], BF16, kind="ExternalInput").ap()
    onecol32_d = dt("onecol32", [1, 128], F32, kind="ExternalInput").ap()
    yout = dt("yout", [TOW, D], F32, kind="ExternalOutput").ap()

    modsl = dt("modsl", [1536], F32)
    modfl = dt("modfl", [12288], F32)
    kvina = dt("kvina", [2048, TOW], BF16)
    kvinb = dt("kvinb", [2048, TOW], BF16)
    agkva = dt("agkva", [NCORES * 2048, TOW], BF16, addr_space="Shared")
    agkvb = dt("agkvb", [NCORES * 2048, TOW], BF16, addr_space="Shared")
    x1d = dt("x1d", [TOW, D], F32)
    wd9 = dt("wd9", [9, TOW], F32)

    if dbg >= 1:
        dbg_y = dt("dbg_y", [D, TOW], F32, kind="ExternalOutput").ap()
        dbg_x1 = dt("dbg_x1", [TOW, D], F32, kind="ExternalOutput").ap()
        dbg_w = dt("dbg_w", [9, TOW], F32, kind="ExternalOutput").ap()
    if dbg >= 2:
        dbg_h = dt("dbg_h", [D, TOW], F32, kind="ExternalOutput").ap()
        dbg_q = dt("dbg_q", [D, TOW], F32, kind="ExternalOutput").ap()
        dbg_kv = dt("dbg_kv", [4096, TOW], BF16, kind="ExternalOutput").ap()
        dbg_h2 = dt("dbg_h2", [D, TOW], F32, kind="ExternalOutput").ap()

    es = ExitStack()
    sb = lambda n, s, d: es.enter_context(nc.sbuf_tensor(n, s, d))
    pt = lambda n, s, d: es.enter_context(nc.psum_tensor(n, s, d))

    with es:
        wring = sb("wring", [128, WRING * 2048], BF16)
        modv = sb("modv", [128, 96], F32)
        ct = sb("ct", [128, 16], F32)
        iden16 = sb("iden16s", [128, 128], BF16)
        iden32 = sb("iden32s", [128, 128], F32)
        ones128 = sb("ones128s", [128, 1], BF16)
        e0mat = sb("e0mats", [128, 128], BF16)
        onecol32 = sb("onecol32s", [1, 128], F32)
        epscol = sb("epscol", [128, 1], F32)
        stat = sb("stat", [128, 16], F32)
        bnbuf = sb("bnbuf", [128, 48], F32)
        dbst = sb("dbst", [128, 1024], F32) if dbg else None

        pb0 = pt("pb0", [128, 512], F32)
        pb1 = pt("pb1", [128, 512], F32)
        pb2 = pt("pb2", [128, 512], F32)
        psc = pt("psc", [128, 2048], F32)
        pT = pt("pT", [128, 1024], BF16)
        obanks = [pb0, pb1]
        rot = [(psc[:, 512 * j:512 * (j + 1)], f"sc{j}") for j in range(4)]
        rot9 = [(psc[0:9, 512 * j:512 * (j + 1)], f"sc{j}") for j in range(4)]

        p = Prog(nc, es)
        for s_ in ["wl0", "wl1", "wl2", "wl3", "wml", "wc", "gmm",
                   "xl0", "xl1", "xl2", "xl3", "xl4", "xl5", "xl6", "xl7", "tbl",
                   "tbl2", "s0", "v0", "t0", "modw",
                   "cc1", "cc2", "cc3", "modr", "lnv", "lns", "lnt", "lne",
                   "kvw0", "kvw1", "kvw2", "kvw3",
                   "kvw4", "kvw5", "kvw6", "kvw7",
                   "gev", "sev", "kld0", "kld1", "kld2", "kld3",
                   "vld0", "vld1", "vld2", "vld3", "rkl0", "rkl1",
                   "qk", "av", "ex", "pv",
                   "oev", "lcv", "bcm", "x1w0", "x1w1", "wd9w", "xr0", "xr1",
                   "wrl", "gtv",
                   "gts", "asv", "auv", "aev", "dev", "cmb", "fin",
                   "outw0", "outw1", "x1r0", "x1r1", "dbgw"]:
            p.sem(s_)

        bank_free = {}

        def bank_wait(eng, bname):
            if bname in bank_free:
                sem, val = bank_free[bname]
                p.wait(eng, sem, val)

        def bank_release(bname, sem):
            bank_free[bname] = (sem, p.sem_n[sem])

        p.dma("sync", iden16[:, :], iden16_d, "tbl")
        p.dma("sync", iden32[:, :], iden32_d, "tbl")
        p.dma("sync", ones128[:, :], ones128_d, "tbl")
        p.dma("sync", e0mat[:, :], e0mat_d, "tbl")
        p.dma("sync", onecol32[:, :], onecol32_d, "tbl")
        p.dma("sync", ct[:, :], cin.rearrange("(a b) -> b a", b=128), "tbl", slow=True)
        ntbl0 = p.sem_n["tbl"]
        p.wait("vector", "tbl", ntbl0)
        p.emit("vector", lambda v: v.memset(epscol[:, :], 1e-5), inc="v0")
        p.wait("scalar", "tbl", ntbl0)
        p.emit("scalar", lambda s: s.activation(ct[:, :], ct[:, :], AF.Silu), inc="s0")

        es_h1 = ExitStack()
        h_fm = es_h1.enter_context(nc.sbuf_tensor("h_fm", [128, KC, TOW], BF16, side="right"))
        es1 = ExitStack()
        sb1 = lambda n, s, d: es1.enter_context(nc.sbuf_tensor(n, s, d))
        wmr = sb1("wmr", [128, 8192], F32)
        modst = sb1("modst", [1, 1536], F32)
        x_sb = sb1("x_sb", [128, 8 * 2048], F32)
        z_sb = sb1("z_sb", [128, 2 * 2048], BF16)

        for cg in range(3):
            if cg > 0:
                p.wait("gpsimd", "t0", p.sem_n["t0"])
            p.dma("gpsimd", wmr[:, :], wmods[cg, :, :], "wml")
            p.wait("tensor", "wml", p.sem_n["wml"])
            p.wait("tensor", "s0", p.sem_n["s0"])
            bap, bname0 = rot[cg % 2]
            bank_wait("tensor", bname0)
            for k in range(KC):
                p.emit("tensor", lambda t, b=bap, k=k: t.matmul(
                    b[0:1, :], ct[:, k:k + 1], wmr[:, k * 512:(k + 1) * 512],
                    start=(k == 0), stop=(k == KC - 1)),
                    inc=("t0" if k == KC - 1 else None))
            p.wait("vector", "t0", p.sem_n["t0"])
            p.emit("vector", lambda v, b=bap, cg=cg: v.tensor_copy(
                modst[0:1, cg * 512:(cg + 1) * 512], b[0:1, :]), inc="v0")
            bank_release(bname0, "v0")
        p.wait("sync", "v0", p.sem_n["v0"])
        p.dma("sync", modsl.ap().rearrange("(a b) -> a b", a=1), modst[0:1, :], "modw")
        p.wait("gpsimd", "modw", p.sem_n["modw"])
        p.emit("gpsimd", lambda g: g.collective_compute(
            "AllGather", AluOpType.bypass, replica_groups=[list(range(NCORES))],
            ins=[modsl.ap().opt()], outs=[modfl.ap().opt()]), inc="cc1")
        p.wait("sync", "cc1", 1)
        for i in range(6):
            p.dma("sync", modv[:, 16 * i:16 * (i + 1)],
                  modfl.ap()[i * 2048:(i + 1) * 2048].rearrange("(a b) -> b a", b=128),
                  "modr", slow=True)
        nmodr = p.sem_n["modr"]
        p.wait("vector", "modr", nmodr)
        p.emit("vector", lambda v: v.tensor_scalar(
            modv[:, 16:32], modv[:, 16:32], 1.0, None, AluOpType.add), inc="v0")
        p.emit("vector", lambda v: v.tensor_scalar(
            modv[:, 64:80], modv[:, 64:80], 1.0, None, AluOpType.add), inc="v0")
        for base in (32, 80):
            p.wait("scalar", "modr", nmodr)
            p.wait("scalar", "s0", p.sem_n["s0"])
            p.wait("scalar", "v0", p.sem_n["v0"])
            p.emit("scalar", lambda s, b=base: s.activation(
                bnbuf[:, 0:16], modv[:, b:b + 16], AF.Abs), inc="s0")
            p.wait("vector", "s0", p.sem_n["s0"])
            p.emit("vector", lambda v: v.tensor_scalar(
                bnbuf[:, 0:16], bnbuf[:, 0:16], 0.25, None, AluOpType.max), inc="v0")
            p.wait("scalar", "v0", p.sem_n["v0"])
            p.emit("scalar", lambda s, b=base: s.activation(
                modv[:, b:b + 16], modv[:, b:b + 16], AF.Sign), inc="s0")
            p.wait("vector", "s0", p.sem_n["s0"])
            p.wait("vector", "v0", p.sem_n["v0"])
            p.emit("vector", lambda v, b=base: v.tensor_tensor(
                modv[:, b:b + 16], modv[:, b:b + 16], bnbuf[:, 0:16],
                AluOpType.mult), inc="v0")
        nmod_v = p.sem_n["v0"]

        for i in range(8):
            p.dma("sync", x_sb[:, i * 2048:(i + 1) * 2048],
                  xin[i * 128:(i + 1) * 128, :], f"xl{i}")

        def emit_ln(xt_ap, z_slice, hdst, gcol, scol, stat_waits, z_waits,
                    evac_waits, par=0):
            so = par * 8
            bo = par * 24
            for (sem, val) in stat_waits:
                p.wait("vector", sem, val)
            p.wait("vector", "lnv", p.sem_n["lnv"])
            p.wait("vector", "v0", p.sem_n["v0"])
            for j in range(4):
                p.emit("vector", lambda v, j=j: v.bn_stats(
                    bnbuf[:, bo + 6 * j:bo + 6 * (j + 1)],
                    xt_ap[:, 512 * j:512 * (j + 1)]), inc="lnv")
            p.wait("vector", "lnv", p.sem_n["lnv"])
            p.emit("vector", lambda v: v.bn_aggr(
                stat[:, so:so + 2], bnbuf[:, bo:bo + 24]), inc="lnv")
            p.wait("scalar", "lnv", p.sem_n["lnv"])
            p.emit("scalar", lambda s: s.activation(
                stat[:, so + 2:so + 3], stat[:, so + 1:so + 2], AF.Sqrt,
                bias=epscol[:, 0:1]), inc="lns")
            p.wait("vector", "lns", p.sem_n["lns"])
            p.emit("vector", lambda v: v.reciprocal(
                stat[:, so + 3:so + 4], stat[:, so + 2:so + 3]), inc="lnv")
            p.wait("vector", "lnv", p.sem_n["lnv"])
            p.emit("vector", lambda v: v.tensor_tensor(
                stat[:, so + 4:so + 5], stat[:, so:so + 1],
                stat[:, so + 3:so + 4], AluOpType.mult), inc="lnv")
            p.wait("vector", "lnv", p.sem_n["lnv"])
            p.emit("vector", lambda v: v.tensor_scalar(
                stat[:, so + 4:so + 5], stat[:, so + 4:so + 5], -1.0, None,
                AluOpType.mult), inc="lnv")
            p.wait("scalar", "lnv", p.sem_n["lnv"])
            for (sem, val) in stat_waits + z_waits:
                p.wait("scalar", sem, val)
            p.emit("scalar", lambda s: s.activation(
                z_slice, xt_ap, AF.Identity, bias=stat[:, so + 4:so + 5],
                scale=stat[:, so + 3:so + 4]), inc="lns")
            ns = p.sem_n["lns"]
            for half in range(2):
                p.wait("tensor", "lns", ns)
                bank_wait("tensor", "pT")
                for j in range(8):
                    kc = half * 8 + j
                    p.emit("tensor", lambda t, j=j, kc=kc: t.matmul(
                        pT[:, 128 * j:128 * (j + 1)],
                        z_slice[:, 128 * kc:128 * (kc + 1)], iden16[:, :],
                        start=(j == 0), stop=(j == 7), is_transpose=True),
                        inc=("lnt" if j == 7 else None))
                p.wait("vector", "lnt", p.sem_n["lnt"])
                p.wait("vector", "v0", nmod_v)
                for (sem, val) in evac_waits:
                    p.wait("vector", sem, val)
                for j in range(8):
                    kc = half * 8 + j
                    dst = hdst(kc)
                    p.emit("vector", lambda v, j=j, d=dst, kc=kc: v.tensor_scalar(
                        d, pT[:, 128 * j:128 * (j + 1)],
                        modv[:, gcol * 16 + kc:gcol * 16 + kc + 1],
                        modv[:, scol * 16 + kc:scol * 16 + kc + 1],
                        AluOpType.mult, AluOpType.add),
                        inc=("lne" if j == 7 else None))
                bank_release("pT", "lne")

        zfree = [("lnt", 0), ("lnt", 0)]
        for i in range(8):
            zsl = z_sb[:, (i % 2) * 2048:(i % 2 + 1) * 2048]
            emit_ln(x_sb[:, i * 2048:(i + 1) * 2048], zsl,
                    lambda kc, i=i: h_fm[:, kc, 128 * i:128 * (i + 1)],
                    1, 0, [(f"xl{i}", 16)], [zfree[i % 2]], [], par=i % 2)
            zfree[i % 2] = ("lnt", p.sem_n["lnt"])
        n_h = p.sem_n["lne"]
        es1.close()

        if dbg >= 2:
            for kc in range(KC):
                p.wait("vector", "lne", n_h)
                p.wait("vector", "dbgw", p.sem_n["dbgw"])
                p.emit("vector", lambda v, kc=kc: v.tensor_copy(
                    dbst[:, :], h_fm[:, kc, :]), inc="v0")
                p.wait("sync", "v0", p.sem_n["v0"])
                p.dma("sync", dbg_h[128 * kc:128 * (kc + 1), :], dbst[:, :], "dbgw")

        wg_state = {"i": 0}
        wc_of_group = {}

        def wload(src_ap, width):
            gi = wg_state["i"]
            wg_state["i"] += 1
            slot = gi % WRING
            wsem = f"wl{slot}"
            if gi >= WRING:
                p.wait("gpsimd", "wc", wc_of_group[gi - WRING])
            p.dma("gpsimd", wring[:, slot * 2048:slot * 2048 + width], src_ap, wsem)
            return gi, slot, (wsem, p.sem_n[wsem])

        def wdone(gi):
            wc_of_group[gi] = p.sem_n["wc"]

        def proj_group(src_ap, width, Kc, Gm, rhs_fn, banks, evac_fn, rhs_waits=(),
                       rhs_waits_tt=None):
            gi, slot, wlv = wload(src_ap, width)
            p.wait("tensor", wlv[0], wlv[1])
            for (sem, val) in rhs_waits:
                p.wait("tensor", sem, val)
            for tt in range(2):
                if rhs_waits_tt is not None:
                    for (sem, val) in rhs_waits_tt[tt]:
                        p.wait("tensor", sem, val)
                bnk, bname = banks[tt]
                bank_wait("tensor", bname)
                for k in range(Kc):
                    last = k == Kc - 1
                    wap = wring[:, slot * 2048 + k * Gm:slot * 2048 + (k + 1) * Gm]
                    rap = rhs_fn(k, tt)
                    p.emit("tensor", lambda t, b=bnk, w=wap, r=rap, st=(k == 0),
                           sp=last: t.matmul(b, w, r, start=st, stop=sp),
                           inc=("wc" if (last and tt == 1) else
                                ("gmm" if last else None)))
                if tt == 1:
                    wdone(gi)
                    gv = ("wc", p.sem_n["wc"])
                else:
                    gv = ("gmm", p.sem_n["gmm"])
                evac_fn(tt, bnk, bname, gv)

        es2 = ExitStack()
        sb2 = lambda n, s, d: es2.enter_context(nc.sbuf_tensor(n, s, d))
        y_fm = sb2("y_fm", [128, KC, TOW], BF16)
        es2a = ExitStack()
        sb2a = lambda n, s, d: es2a.enter_context(nc.sbuf_tensor(n, s, d))
        q_fm = sb2a("q_fm", [128, H, TOW], BF16)
        kvst = sb2a("kvst", [128, 4 * 512], BF16)
        kv_seq = {"i": 0}
        kvst_free = [None] * 4

        def emit_kv_group(fo, grp):
            def evac(tt, bnk, bname, gv, fo=fo, grp=grp):
                sem, val = gv
                n = kv_seq["i"]
                kv_seq["i"] += 1
                sl = n % 4
                ksem = f"kvw{sl + 4 * grp}"
                p.wait("scalar", sem, val)
                if kvst_free[sl] is not None:
                    p.wait("scalar", kvst_free[sl][0], kvst_free[sl][1])
                p.emit("scalar", lambda s, b=bnk, sl=sl: s.copy(
                    kvst[:, sl * 512:(sl + 1) * 512], b[:, :]), inc="sev")
                bank_release(bname, "sev")
                p.wait("sync", "sev", p.sem_n["sev"])
                h9 = (fo - 16) % 8
                dst = kvina if grp == 0 else kvinb
                row = 128 * h9 if fo < 32 else 1024 + 128 * h9
                p.dma("sync", dst.ap()[row:row + 128, tt * 512:(tt + 1) * 512],
                      kvst[:, sl * 512:(sl + 1) * 512], ksem)
                kvst_free[sl] = (ksem, p.sem_n[ksem])

            bsel = [rot[(2 * fo + t) % 4] for t in range(2)]
            proj_group(wq_d[fo, :, :], 2048, KC, 128,
                       lambda k, tt: h_fm[:, k, tt * 512:(tt + 1) * 512],
                       bsel, evac, rhs_waits=[("lne", n_h)])

        def emit_q_group(fo):
            def evac(tt, bnk, bname, gv, fo=fo):
                sem, val = gv
                p.wait("vector", sem, val)
                p.emit("vector", lambda v, b=bnk, fo=fo, tt=tt: v.tensor_scalar(
                    q_fm[:, fo, tt * 512:(tt + 1) * 512], b[:, :], QSC, None,
                    AluOpType.mult), inc="gev")
                bank_release(bname, "gev")

            bsel = [rot[(2 * fo + t) % 4] for t in range(2)]
            proj_group(wq_d[fo, :, :], 2048, KC, 128,
                       lambda k, tt: h_fm[:, k, tt * 512:(tt + 1) * 512],
                       bsel, evac, rhs_waits=[("lne", n_h)])

        for fo in list(range(16, 24)) + list(range(32, 40)):
            emit_kv_group(fo, 0)
        kvw_tot = [0] * 8
        for i in range(4):
            kvw_tot[i] = p.sem_n[f"kvw{i}"]
            p.wait("gpsimd", f"kvw{i}", kvw_tot[i])
        p.emit("gpsimd", lambda g: g.collective_compute(
            "AllGather", AluOpType.bypass, replica_groups=[list(range(NCORES))],
            ins=[kvina.ap().opt()], outs=[agkva.ap().opt()]), inc="cc2")
        for fo in list(range(24, 32)) + list(range(40, 48)):
            emit_kv_group(fo, 1)
        for i in range(4):
            kvw_tot[4 + i] = p.sem_n[f"kvw{4 + i}"]
            p.wait("gpsimd", f"kvw{4 + i}", kvw_tot[4 + i])
        p.emit("gpsimd", lambda g: g.collective_compute(
            "AllGather", AluOpType.bypass, replica_groups=[list(range(NCORES))],
            ins=[kvinb.ap().opt()], outs=[agkvb.ap().opt()]), inc="cc3")
        for fo in range(16):
            emit_q_group(fo)
        n_q = p.sem_n["gev"]
        es_h1.close()

        if dbg >= 2:
            for kc in range(KC):
                p.wait("vector", "gev", n_q)
                p.wait("vector", "dbgw", p.sem_n["dbgw"])
                p.emit("vector", lambda v, kc=kc: v.tensor_copy(
                    dbst[:, :], q_fm[:, kc, :]), inc="v0")
                p.wait("sync", "v0", p.sem_n["v0"])
                p.dma("sync", dbg_q[128 * kc:128 * (kc + 1), :], dbst[:, :], "dbgw")
            for i in range(8):
                p.wait("sync", f"kvw{i}", kvw_tot[i])
            p.dma("sync", dbg_kv[0:2048, :], kvina.ap()[:, :], "dbgw")
            p.dma("sync", dbg_kv[2048:4096, :], kvinb.ap()[:, :], "dbgw")

        tri = sb2a("tri_s", [128, 4 * 512], F32)
        rk4l = sb2a("rk4l_s", [128, H * 4 * 128], BF16)
        kbuf = sb2a("kbuf", [128, 4 * 512], BF16)
        vbuf = sb2a("vbuf", [128, 4 * 512], BF16)
        rkb = sb2a("rkb", [128, 2 * 16384], BF16)
        ptb = sb2a("ptb", [128, 4 * 1024], BF16)
        lbuf = sb2a("lbuf", [1, 4 * 512], F32)
        lbc = sb2a("lbc", [128, 2 * 512], F32)
        p.dma("sync", tri[:, :], tri_d, "tbl2")
        p.dma("sync", rk4l[:, :], rk4l_d, "tbl2")
        ntbl1 = p.sem_n["tbl2"]

        kv_ld = {"i": 0}
        ptb_i = {"i": 0}
        sc_i = {"i": 0}
        kfree = [("qk", 0)] * 4
        vfree = [("pv", 0)] * 4
        ptfree = [("pv", 0)] * 4
        rkfree = [("qk", 0), ("qk", 0)]
        ofree = [("oev", 0), ("oev", 0)]
        lfree = [("lcv", 0)]
        lbuf_bc = [0]
        pvq = []
        PVDEPTH = 2
        lbc_last = [0, 0]

        for h in range(H):
            rslot = h % 2
            sm, vv = rkfree[rslot]
            p.wait("sync", sm, vv)
            rsem = f"rkl{rslot}"
            p.dma("sync", rkb[:, rslot * 16384:(rslot + 1) * 16384],
                  rk4r_d[h, :, :], rsem)
            rk_ready = (rsem, p.sem_n[rsem])

            for s in range(2):
                unit = 2 * h + s
                ob = unit % 2
                obank, obname = obanks[ob], f"pb{ob}"
                lbank, lbname = pb2, "pb2"
                nkv = (7 if s == 0 else 15)
                it_list = [("d", s)] + [("k", j) for j in range(nkv)]
                grp = 0 if h < 8 else 1
                h9 = h % 8
                ccsem = "cc2" if grp == 0 else "cc3"
                for (kind, j) in it_list:
                    kslot = kv_ld["i"] % 4
                    kv_ld["i"] += 1
                    if kind == "d":
                        src, cc_w = (kvina if grp == 0 else kvinb).ap(), False
                        krow, vrow, col = 128 * h9, 1024 + 128 * h9, s * 512
                        itidx = 15
                    else:
                        own, half = _own(j)
                        src, cc_w = (agkva if grp == 0 else agkvb).ap(), True
                        krow = 2048 * own + 128 * h9
                        vrow = 2048 * own + 1024 + 128 * h9
                        col = half * 512
                        itidx = j
                    sm, vv = kfree[kslot]
                    p.wait("sync", sm, vv)
                    if cc_w:
                        p.wait("sync", ccsem, 1)
                    else:
                        for q4 in range(4):
                            p.wait("sync", f"kvw{q4 + 4 * grp}", kvw_tot[q4 + 4 * grp])
                    ksm = f"kld{kslot}"
                    p.dma("sync", kbuf[:, kslot * 512:(kslot + 1) * 512],
                          src[krow:krow + 128, col:col + 512], ksm)
                    kv_val = (ksm, p.sem_n[ksm])
                    sm, vv = vfree[kslot]
                    p.wait("sync", sm, vv)
                    for sub in range(4):
                        p.dma("sync",
                              vbuf[:, kslot * 512 + sub * 128:
                                   kslot * 512 + (sub + 1) * 128],
                              src[vrow:vrow + 128,
                                  col + sub * 128:col + (sub + 1) * 128],
                              f"vld{kslot}", transpose=True)
                    vv_val = (f"vld{kslot}", p.sem_n[f"vld{kslot}"])

                    jsl = s * 16 + itidx
                    blk, rbase = jsl, 0
                    first = kind == "d"
                    last = kind == "k" and j == nkv - 1
                    for half2 in range(2):
                        scs = sc_i["i"] % 2
                        sc_i["i"] += 1
                        scap = psc[:, scs * 1024:(scs + 1) * 1024]
                        scn0, scn1 = f"sc{2 * scs}", f"sc{2 * scs + 1}"
                        p.wait("tensor", kv_val[0], kv_val[1])
                        p.wait("tensor", rk_ready[0], rk_ready[1])
                        p.wait("tensor", "gev", n_q)
                        p.wait("tensor", "tbl2", ntbl1)
                        bank_wait("tensor", scn0)
                        bank_wait("tensor", scn1)
                        qap = q_fm[:, h, s * 512:(s + 1) * 512]
                        for s2 in range(2):
                            sub = 2 * half2 + s2
                            sap = scap[:, s2 * 512:(s2 + 1) * 512]
                            kap = kbuf[:, kslot * 512 + sub * 128:
                                       kslot * 512 + (sub + 1) * 128]
                            p.emit("tensor", lambda t, b=sap, k=kap, q=qap: t.matmul(
                                b, k, q, start=True, stop=False))
                            lap = rk4l[:,
                                       (h * 4 + sub) * 128:(h * 4 + sub + 1) * 128]
                            rap = rkb[:,
                                      rslot * 16384 + blk * 512:
                                      rslot * 16384 + (blk + 1) * 512]
                            p.emit("tensor", lambda t, b=sap, l=lap, r=rap: t.matmul(
                                b, l, r, start=False, stop=True), inc="qk")
                        nqk = p.sem_n["qk"]
                        if first:
                            p.wait("vector", "qk", nqk)
                            p.wait("vector", "tbl2", ntbl1)
                            p.wait("vector", "av", p.sem_n["av"])
                            p.emit("vector", lambda v, b=scap, half2=half2:
                                   v.tensor_tensor(
                                       b, b, tri[:, half2 * 1024:(half2 + 1) * 1024],
                                       AluOpType.add), inc="av")
                        pslot = ptb_i["i"] % 4
                        ptb_i["i"] += 1
                        sm, vv = ptfree[pslot]
                        p.wait("scalar", sm, vv)
                        p.wait("scalar", "qk", nqk)
                        if first:
                            p.wait("scalar", "av", p.sem_n["av"])
                        p.emit("scalar", lambda sc, b=scap, ps=pslot: sc.activation(
                            ptb[:, ps * 1024:(ps + 1) * 1024], b, AF.Exp),
                            inc="ex")
                        bank_release(scn0, "ex")
                        bank_release(scn1, "ex")
                        nex = p.sem_n["ex"]

                        def pv_emit(obank=obank, lbank=lbank, kslot=kslot,
                                    half2=half2, pslot=pslot, first=first,
                                    last=last, nex=nex, vv_val=vv_val,
                                    owait=(ofree[ob] if (first and half2 == 0)
                                           else None),
                                    lwait=(lfree[0] if (first and half2 == 0)
                                           else None),
                                    rslot=rslot, nqk=nqk):
                            p.wait("tensor", "ex", nex)
                            p.wait("tensor", vv_val[0], vv_val[1])
                            if owait is not None:
                                p.wait("tensor", owait[0], owait[1])
                                p.wait("tensor", lwait[0], lwait[1])
                            for s2 in range(2):
                                sub = 2 * half2 + s2
                                st = first and sub == 0
                                sp = last and sub == 3
                                vap = vbuf[:, kslot * 512 + sub * 128:
                                           kslot * 512 + (sub + 1) * 128]
                                pap = ptb[:, pslot * 1024 + s2 * 512:
                                          pslot * 1024 + (s2 + 1) * 512]
                                p.emit("tensor", lambda t, o=obank, v=vap, pp=pap,
                                       st=st, sp=sp:
                                       t.matmul(o[:, :], v, pp, start=st, stop=sp))
                                p.emit("tensor", lambda t, lb=lbank, pp=pap,
                                       st=st, sp=sp:
                                       t.matmul(lb[:, :], e0mat[:, :], pp,
                                                start=st, stop=sp), inc="pv")
                            ptfree[pslot] = ("pv", p.sem_n["pv"])
                            if half2 == 1:
                                vfree[kslot] = ("pv", p.sem_n["pv"])

                        pvq.append(pv_emit)
                        if len(pvq) > PVDEPTH:
                            pvq.pop(0)()
                        if half2 == 1:
                            kfree[kslot] = ("qk", p.sem_n["qk"])
                            rkfree[rslot] = ("qk", p.sem_n["qk"])

                while pvq:
                    pvq.pop(0)()
                lc = 0
                p.wait("vector", "pv", p.sem_n["pv"])
                p.wait("vector", "bcm", lbuf_bc[0])
                p.wait("vector", "lcv", p.sem_n["lcv"])
                p.emit("vector", lambda v, lb=lbank, lc=lc: v.tensor_copy(
                    lbuf[0:1, lc:lc + 512], lb[0:1, :]), inc="lcv")
                p.wait("vector", "lcv", p.sem_n["lcv"])
                p.emit("vector", lambda v, lc=lc: v.reciprocal(
                    lbuf[0:1, lc + 512:lc + 1024], lbuf[0:1, lc:lc + 512]),
                    inc="lcv")
                p.wait("tensor", "lcv", p.sem_n["lcv"])
                p.emit("tensor", lambda t, lb=lbank, lc=lc: t.matmul(
                    lb[:, :], onecol32[0:1, :], lbuf[0:1, lc + 512:lc + 1024],
                    start=True, stop=True), inc="bcm")
                lbuf_bc[0] = p.sem_n["bcm"]
                p.wait("vector", "bcm", p.sem_n["bcm"])
                p.wait("vector", "oev", lbc_last[ob])
                p.emit("vector", lambda v, lb=lbank, ob=ob: v.tensor_copy(
                    lbc[:, ob * 512:(ob + 1) * 512], lb[:, :]), inc="lcv")
                lfree[0] = ("lcv", p.sem_n["lcv"])
                p.wait("vector", "lcv", p.sem_n["lcv"])
                p.emit("vector", lambda v, o=obank, h=h, s=s, ob=ob: v.tensor_tensor(
                    y_fm[:, h, s * 512:(s + 1) * 512], o[:, :],
                    lbc[:, ob * 512:(ob + 1) * 512], AluOpType.mult), inc="oev")
                ofree[ob] = ("oev", p.sem_n["oev"])
                lbc_last[ob] = p.sem_n["oev"]
        n_attn = p.sem_n["oev"]
        es2a.close()
        h2_fm = es.enter_context(nc.sbuf_tensor("h2_fm", [128, KC, TOW], BF16, side="right"))

        if dbg >= 1:
            for kc in range(KC):
                p.wait("vector", "oev", n_attn)
                p.wait("vector", "dbgw", p.sem_n["dbgw"])
                p.emit("vector", lambda v, kc=kc: v.tensor_copy(
                    dbst[:, :], y_fm[:, kc, :]), inc="v0")
                p.wait("sync", "v0", p.sem_n["v0"])
                p.dma("sync", dbg_y[128 * kc:128 * (kc + 1), :], dbst[:, :], "dbgw")

        es2b = ExitStack()
        sb2b = lambda n, s, d: es2b.enter_context(nc.sbuf_tensor(n, s, d))
        wo_tmp = sb2b("wo_tmp", [128, KC, TOW], BF16)
        xre = sb2b("xre", [128, 2 * 2048], F32)
        x1t = sb2b("x1t", [128, 2 * 2048], F32)
        z2_sb = sb2b("z2_sb", [128, 2 * 2048], BF16)

        for fo in range(16):
            def evac(tt, bnk, bname, gv, fo=fo):
                sem, val = gv
                p.wait("vector", sem, val)
                p.wait("vector", "v0", nmod_v)
                p.emit("vector", lambda v, b=bnk, fo=fo, tt=tt: v.tensor_scalar(
                    wo_tmp[:, fo, tt * 512:(tt + 1) * 512], b[:, :],
                    modv[:, 32 + fo:33 + fo], None, AluOpType.mult), inc="gev")
                bank_release(bname, "gev")

            bsel = [rot[(2 * fo + t) % 4] for t in range(2)]
            proj_group(wo_d[fo, :, :], 2048, KC, 128,
                       lambda k, tt: y_fm[:, k, tt * 512:(tt + 1) * 512],
                       bsel, evac, rhs_waits=[("oev", n_attn)])
        n_wo = p.sem_n["gev"]

        xre_free = [("lnv", 0), ("lnv", 0)]
        x1t_free = [[], []]
        z2free = [("lnt", 0), ("lnt", 0)]
        for i in range(8):
            ii = i % 2
            sm, vv = xre_free[ii]
            p.wait("sync", sm, vv)
            xsm2 = f"xr{ii}"
            p.dma("sync", xre[:, ii * 2048:(ii + 1) * 2048],
                  xin[i * 128:(i + 1) * 128, :], xsm2)
            nxr = (xsm2, p.sem_n[xsm2])
            for half in range(2):
                p.wait("tensor", "gev", n_wo)
                bank_wait("tensor", "pT")
                for jj in range(8):
                    kc = half * 8 + jj
                    p.emit("tensor", lambda t, j=jj, kc=kc, i=i: t.matmul(
                        pT[:, 128 * j:128 * (j + 1)],
                        wo_tmp[:, kc, 128 * i:128 * (i + 1)], iden16[:, :],
                        start=(j == 0), stop=(j == 7), is_transpose=True),
                        inc=("lnt" if jj == 7 else None))
                p.wait("vector", "lnt", p.sem_n["lnt"])
                p.wait("vector", nxr[0], nxr[1])
                for (sem, val) in x1t_free[ii]:
                    p.wait("vector", sem, val)
                p.emit("vector", lambda v, half=half, ii=ii: v.tensor_tensor(
                    x1t[:, ii * 2048 + half * 1024:ii * 2048 + (half + 1) * 1024],
                    pT[:, :],
                    xre[:, ii * 2048 + half * 1024:ii * 2048 + (half + 1) * 1024],
                    AluOpType.add), inc="lnv")
                bank_release("pT", "lnv")
            nadd = p.sem_n["lnv"]
            xre_free[ii] = ("lnv", nadd)
            xsem = f"x1w{ii}"
            p.wait("sync", "lnv", nadd)
            p.dma("sync", x1d.ap()[i * 128:(i + 1) * 128, :],
                  x1t[:, ii * 2048:(ii + 1) * 2048], xsem)
            if dbg >= 1:
                p.dma("sync", dbg_x1[i * 128:(i + 1) * 128, :],
                      x1t[:, ii * 2048:(ii + 1) * 2048], "dbgw")
            emit_ln(x1t[:, ii * 2048:(ii + 1) * 2048],
                    z2_sb[:, ii * 2048:(ii + 1) * 2048],
                    lambda kc, i=i: h2_fm[:, kc, 128 * i:128 * (i + 1)],
                    4, 3, [("lnv", nadd)], [z2free[ii]],
                    [("oev", n_attn)], par=ii)
            if i == 3:
                lne_half = p.sem_n["lne"]
            z2free[ii] = ("lnt", p.sem_n["lnt"])
            x1t_free[ii] = [(xsem, p.sem_n[xsem]), ("lns", p.sem_n["lns"])]
            if dbg >= 1:
                x1t_free[ii].append(("dbgw", p.sem_n["dbgw"]))
        n_h2 = p.sem_n["lne"]
        es2b.close()
        es2.close()

        if dbg >= 2:
            for kc in range(KC):
                p.wait("vector", "lne", n_h2)
                p.wait("vector", "dbgw", p.sem_n["dbgw"])
                p.emit("vector", lambda v, kc=kc: v.tensor_copy(
                    dbst[:, :], h2_fm[:, kc, :]), inc="v0")
                p.wait("sync", "v0", p.sem_n["v0"])
                p.dma("sync", dbg_h2[128 * kc:128 * (kc + 1), :], dbst[:, :], "dbgw")

        es3 = ExitStack()
        sb3 = lambda n, s, d: es3.enter_context(nc.sbuf_tensor(n, s, d))
        bcw = sb3("bcw", [128, 9 * 1024], BF16)
        es3a = ExitStack()
        sb3a = lambda n, s, d: es3a.enter_context(nc.sbuf_tensor(n, s, d))
        g_sb = sb3a("g_sb", [9, TOW], F32)
        gtile = sb3a("gtile", [128, 8, 9], F32)
        gw = sb3a("gw", [128, 8, 8], F32)
        gpr = sb3a("gpr", [128, 8, 8], F32)
        gpn = sb3a("gpn", [128, 8, 8], F32)
        wrow = sb3a("wrow", [1, 9 * TOW], F32)

        p.wait("vector", "lnv", p.sem_n["lnv"])

        def evac_g(tt, bnk, bname, gv):
            sem, val = gv
            p.wait("vector", sem, val)
            p.wait("vector", "gtv", p.sem_n["gtv"])
            p.emit("vector", lambda v, b=bnk, tt=tt: v.tensor_copy(
                g_sb[0:9, tt * 512:(tt + 1) * 512], b), inc="gtv")
            bank_release(bname, "gtv")

        proj_group(g9_d[:, :], 144, KC, 9,
                   lambda k, tt: h2_fm[:, k, tt * 512:(tt + 1) * 512],
                   [rot9[0], rot9[1]],
                   evac_g, rhs_waits_tt=([("lne", lne_half)], [("lne", n_h2)]))

        for half in range(2):
            p.wait("tensor", "gtv", p.sem_n["gtv"])
            bank_wait("tensor", "pb2")
            for jj in range(4):
                i = 4 * half + jj
                p.emit("tensor", lambda t, i=i, jj=jj: t.matmul(
                    pb2[:, 9 * jj:9 * (jj + 1)], g_sb[0:9, 128 * i:128 * (i + 1)],
                    iden32[0:9, 0:9], start=(jj == 0), stop=(jj == 3),
                    is_transpose=True), inc=("gmm" if jj == 3 else None))
            p.wait("vector", "gmm", p.sem_n["gmm"])
            p.wait("vector", "gtv", p.sem_n["gtv"])
            p.emit("vector", lambda v, half=half: v.tensor_copy(
                gtile[:, 4 * half:4 * (half + 1), :], pb2[:, 0:36]), inc="gtv")
            bank_release("pb2", "gtv")
        p.wait("scalar", "gtv", p.sem_n["gtv"])
        p.wait("scalar", "gts", p.sem_n["gts"])
        p.emit("scalar", lambda s: s.activation(
            gw[:, :, :], gtile[:, :, 0:8], AF.Exp), inc="gts")
        p.wait("vector", "gts", p.sem_n["gts"])
        p.emit("vector", lambda v: v.reduce_sum(
            bnbuf[:, 0:8], gw[:, :, :], axis=AX.X), inc="gtv")
        p.wait("vector", "gtv", p.sem_n["gtv"])
        p.emit("vector", lambda v: v.reciprocal(bnbuf[:, 8:16], bnbuf[:, 0:8]),
               inc="gtv")
        p.wait("vector", "gtv", p.sem_n["gtv"])
        rcp_b = None
        p.emit("vector", lambda v: v.tensor_tensor(
            gpr[:, :, :], gw[:, :, :],
            bnbuf[:, 8:16].rearrange("p (a b) -> p a b", b=1).broadcast_to((128, 8, 8)),
            AluOpType.mult), inc="gtv")
        p.wait("vector", "gtv", p.sem_n["gtv"])
        p.emit("vector", lambda v: v.reduce_max(
            bnbuf[:, 16:24], gpr[:, :, :], axis=AX.X), inc="gtv")
        p.wait("vector", "gtv", p.sem_n["gtv"])
        p.emit("vector", lambda v: v.tensor_tensor(
            gpn[:, :, :], gpr[:, :, :],
            bnbuf[:, 16:24].rearrange("p (a b) -> p a b", b=1).broadcast_to((128, 8, 8)),
            AluOpType.is_ge), inc="gtv")
        p.wait("vector", "gtv", p.sem_n["gtv"])
        p.emit("vector", lambda v: v.tensor_scalar(
            gpn[:, :, :], gpn[:, :, :], NEG, None, AluOpType.mult), inc="gtv")
        p.wait("vector", "gtv", p.sem_n["gtv"])
        p.emit("vector", lambda v: v.tensor_tensor(
            gpn[:, :, :], gpn[:, :, :], gpr[:, :, :], AluOpType.add), inc="gtv")
        p.wait("vector", "gtv", p.sem_n["gtv"])
        p.emit("vector", lambda v: v.reduce_max(
            bnbuf[:, 24:32], gpn[:, :, :], axis=AX.X), inc="gtv")
        p.wait("vector", "gtv", p.sem_n["gtv"])
        p.emit("vector", lambda v: v.tensor_tensor(
            gpn[:, :, :], gpr[:, :, :],
            bnbuf[:, 24:32].rearrange("p (a b) -> p a b", b=1).broadcast_to((128, 8, 8)),
            AluOpType.is_ge), inc="gtv")
        p.wait("vector", "gtv", p.sem_n["gtv"])
        p.emit("vector", lambda v: v.tensor_tensor(
            gtile[:, :, 0:8], gpr[:, :, :], gpn[:, :, :], AluOpType.mult),
            inc="gtv")
        p.wait("scalar", "gtv", p.sem_n["gtv"])
        p.wait("scalar", "gts", p.sem_n["gts"])
        p.emit("scalar", lambda s: s.activation(
            gtile[:, :, 8:9], gtile[:, :, 8:9], AF.Sigmoid), inc="gts")
        for half in range(2):
            p.wait("tensor", "gtv", p.sem_n["gtv"])
            p.wait("tensor", "gts", p.sem_n["gts"])
            bank_wait("tensor", "pb2")
            for jj in range(4):
                i = 4 * half + jj
                p.emit("tensor", lambda t, i=i, jj=jj: t.matmul(
                    pb2[0:9, 128 * jj:128 * (jj + 1)], gtile[:, i, :],
                    iden32[:, :], start=(jj == 0), stop=(jj == 3),
                    is_transpose=True), inc=("gmm" if jj == 3 else None))
            p.wait("vector", "gmm", p.sem_n["gmm"])
            p.wait("vector", "gtv", p.sem_n["gtv"])
            p.emit("vector", lambda v, half=half: v.tensor_copy(
                g_sb[0:9, 512 * half:512 * (half + 1)], pb2[0:9, 0:512]),
                inc="gtv")
            bank_release("pb2", "gtv")
        p.wait("sync", "gtv", p.sem_n["gtv"])
        p.dma("sync", wd9.ap()[:, :], g_sb[0:9, :], "wd9w")
        p.wait("sync", "wd9w", p.sem_n["wd9w"])
        p.dma("sync", wrow[0:1, :], wd9.ap().rearrange("a b -> (a b)").rearrange("(a b) -> a b", a=1), "wrl")
        nwrow = p.sem_n["wrl"]
        if dbg >= 1:
            p.dma("sync", dbg_w[:, :], wd9.ap()[:, :], "dbgw")
        for r in range(9):
            for tt in range(2):
                p.wait("tensor", "wrl", nwrow)
                bank_wait("tensor", "pb2")
                p.emit("tensor", lambda t, r=r, tt=tt: t.matmul(
                    pb2[:, :], onecol32[0:1, :],
                    wrow[0:1, r * 1024 + tt * 512:r * 1024 + (tt + 1) * 512],
                    start=True, stop=True), inc="gmm")
                p.wait("vector", "gmm", p.sem_n["gmm"])
                p.wait("vector", "gtv", p.sem_n["gtv"])
                p.emit("vector", lambda v, r=r, tt=tt: v.tensor_copy(
                    bcw[:, r * 1024 + tt * 512:r * 1024 + (tt + 1) * 512],
                    pb2[:, :]), inc="gtv")
                bank_release("pb2", "gtv")
        n_bcw = p.sem_n["gtv"]
        es3a.close()

        es3b = ExitStack()
        sb3b = lambda n, s, d: es3b.enter_context(nc.sbuf_tensor(n, s, d))
        comb = sb3b("comb", [128, KC * TOW], F32)
        es3b1 = ExitStack()
        sb3b1 = lambda n, s, d: es3b1.enter_context(nc.sbuf_tensor(n, s, d))
        A_p = [sb3b1(f"A{jj}", [128, 11, TOW], BF16) for jj in range(2)]
        gtmp = sb3b1("gtmp", [128, 1024], BF16)
        utmp = sb3b1("utmp", [128, 1024], BF16)

        p.wait("vector", "lne", n_h2)
        p.wait("vector", "cmb", p.sem_n["cmb"])
        p.wait("vector", "x1w0", p.sem_n["x1w0"])
        p.wait("vector", "x1w1", p.sem_n["x1w1"])
        if dbg >= 1:
            p.wait("vector", "dbgw", p.sem_n["dbgw"])
        for kc in range(KC):
            p.emit("vector", lambda v, kc=kc: v.memset(
                comb[:, kc * 1024:(kc + 1) * 1024], 0.0), inc="cmb")
        ncmb0 = p.sem_n["cmb"]

        pieces = [("s", q) for q in range(4)] + [("e", e) for e in range(8)]
        a_last_use = [("wc", 0), ("wc", 0)]

        def build_A(pi, kind, idx):
            Ab = A_p[pi % 2]
            wcol = 8 if kind == "s" else idx
            asm, asv = a_last_use[pi % 2]
            for jj in range(11):
                if kind == "s":
                    gsrc = sgu_d[11 * idx + jj, :, :]
                    usrc = sgu_d[44 + 11 * idx + jj, :, :]
                else:
                    gsrc = w13_d[idx, jj, :, :]
                    usrc = w13_d[idx, 11 + jj, :, :]

                def evg(tt, bnk, bname, gv):
                    sem, val = gv
                    p.wait("scalar", sem, val)
                    p.wait("scalar", "aev", p.sem_n["aev"])
                    p.wait("scalar", "asv", p.sem_n["asv"])
                    p.emit("scalar", lambda s, b=bnk, tt=tt: s.activation(
                        gtmp[:, tt * 512:(tt + 1) * 512], b[:, :], AF.Silu),
                        inc="asv")
                    bank_release(bname, "asv")

                def evu(tt, bnk, bname, gv, wcol=wcol):
                    sem, val = gv
                    p.wait("vector", sem, val)
                    p.wait("vector", "gtv", n_bcw)
                    p.wait("vector", "aev", p.sem_n["aev"])
                    p.wait("vector", "auv", p.sem_n["auv"])
                    p.emit("vector", lambda v, b=bnk, tt=tt, w=wcol: v.tensor_tensor(
                        utmp[:, tt * 512:(tt + 1) * 512], b[:, :],
                        bcw[:, w * 1024 + tt * 512:w * 1024 + (tt + 1) * 512],
                        AluOpType.mult), inc="auv")
                    bank_release(bname, "auv")

                bsel0 = [rot[(2 * jj + t) % 4] for t in range(2)]
                bsel1 = [rot[(2 * jj + 2 + t) % 4] for t in range(2)]
                rw = [(asm, asv)] if jj == 0 else []
                rwtt = ([("lne", lne_half)], [("lne", n_h2)])
                proj_group(gsrc, 2048, KC, 128,
                           lambda k, tt: h2_fm[:, k, tt * 512:(tt + 1) * 512],
                           bsel0, evg, rhs_waits=rw, rhs_waits_tt=rwtt)
                proj_group(usrc, 2048, KC, 128,
                           lambda k, tt: h2_fm[:, k, tt * 512:(tt + 1) * 512],
                           bsel1, evu, rhs_waits_tt=rwtt)
                p.wait("vector", "asv", p.sem_n["asv"])
                p.wait("vector", "auv", p.sem_n["auv"])
                p.wait("vector", asm, asv)
                p.wait("vector", "aev", p.sem_n["aev"])
                p.emit("vector", lambda v, Ab=Ab, jj=jj: v.tensor_tensor(
                    Ab[:, jj, :], gtmp[:, :], utmp[:, :], AluOpType.mult),
                    inc="aev")

        def down_pair(pi0, pc0, pi1, pc1):
            nae = p.sem_n["aev"]
            for fo in range(16):
                for half, (pi, (kind, idx)) in enumerate([(pi0, pc0), (pi1, pc1)]):
                    src = sdn_d[idx, fo, :, :] if kind == "s" else w2_d[idx, fo, :, :]
                    gi, slot, wlv = wload(src, 1408)
                    p.wait("tensor", wlv[0], wlv[1])
                    if half == 0 and fo == 0:
                        p.wait("tensor", "aev", nae)
                    Ab = A_p[pi % 2]
                    for tt in range(2):
                        bnk, bname = obanks[tt], f"pb{tt}"
                        if half == 0:
                            bank_wait("tensor", bname)
                        for k in range(11):
                            wap = wring[:, slot * 2048 + k * 128:
                                        slot * 2048 + (k + 1) * 128]
                            rap = Ab[:, k, tt * 512:(tt + 1) * 512]
                            p.emit("tensor", lambda t, b=bnk, w=wap, r=rap,
                                   st=(half == 0 and k == 0),
                                   sp=(half == 1 and k == 10):
                                   t.matmul(b[:, :], w, r, start=st, stop=sp),
                                   inc=("wc" if (k == 10 and tt == 1) else None))
                    wdone(gi)
                nwc = p.sem_n["wc"]
                for tt in range(2):
                    bnk, bname = obanks[tt], f"pb{tt}"
                    p.wait("vector", "wc", nwc)
                    p.wait("vector", "cmb", ncmb0)
                    p.wait("vector", "dev", p.sem_n["dev"])
                    p.wait("vector", "v0", nmod_v)
                    cap = comb[:, fo * 1024 + tt * 512:fo * 1024 + (tt + 1) * 512]
                    p.emit("vector", lambda v, b=bnk, c=cap, fo=fo:
                           v.scalar_tensor_tensor(
                               c, b[:, :], modv[:, 80 + fo:81 + fo], c,
                               AluOpType.mult, AluOpType.add), inc="dev")
                    bank_release(bname, "dev")
            a_last_use[pi0 % 2] = ("wc", p.sem_n["wc"])
            a_last_use[pi1 % 2] = ("wc", p.sem_n["wc"])

        for pp in range(6):
            build_A(2 * pp, *pieces[2 * pp])
            build_A(2 * pp + 1, *pieces[2 * pp + 1])
            down_pair(2 * pp, pieces[2 * pp], 2 * pp + 1, pieces[2 * pp + 1])
        n_dev = p.sem_n["dev"]
        es3b1.close()

        es3c = ExitStack()
        sb3c = lambda n, s, d: es3c.enter_context(nc.sbuf_tensor(n, s, d))
        x1re = sb3c("x1re", [128, 2 * 2048], F32)
        outt = sb3c("outt", [128, 2 * 2048], F32)
        x1re_free = [("fin", 0), ("fin", 0)]
        n_x1w = [p.sem_n["x1w0"], p.sem_n["x1w1"]]
        for i in range(8):
            ii = i % 2
            sm, vv = x1re_free[ii]
            p.wait("sync", sm, vv)
            p.wait("sync", "x1w0", n_x1w[0])
            p.wait("sync", "x1w1", n_x1w[1])
            p.wait("sync", "dev", n_dev)
            qsm = f"x1r{ii}"
            p.dma("sync", x1re[:, ii * 2048:(ii + 1) * 2048],
                  x1d.ap()[i * 128:(i + 1) * 128, :], qsm)
            nx1r = (qsm, p.sem_n[qsm])
            osem = f"outw{ii}"
            for quad in range(4):
                bnk0, bname = rot[quad]
                bnk = bnk0
                p.wait("tensor", "dev", n_dev)
                bank_wait("tensor", bname)
                for jj in range(4):
                    kc = quad * 4 + jj
                    p.emit("tensor", lambda t, b=bnk, j=jj, kc=kc, i=i: t.matmul(
                        b[:, 128 * j:128 * (j + 1)],
                        comb[:, kc * 1024 + 128 * i:kc * 1024 + 128 * (i + 1)],
                        iden32[:, :], start=(j == 0), stop=(j == 3),
                        is_transpose=True), inc=("lnt" if jj == 3 else None))
                p.wait("vector", "lnt", p.sem_n["lnt"])
                p.wait("vector", nx1r[0], nx1r[1])
                p.wait("vector", osem, p.sem_n[osem])
                p.emit("vector", lambda v, b=bnk, quad=quad, ii=ii: v.tensor_tensor(
                    outt[:, ii * 2048 + quad * 512:ii * 2048 + (quad + 1) * 512],
                    b[:, :],
                    x1re[:, ii * 2048 + quad * 512:ii * 2048 + (quad + 1) * 512],
                    AluOpType.add), inc="fin")
                bank_release(bname, "fin")
            x1re_free[ii] = ("fin", p.sem_n["fin"])
            p.wait("sync", "fin", p.sem_n["fin"])
            p.dma("sync", yout[i * 128:(i + 1) * 128, :],
                  outt[:, ii * 2048:(ii + 1) * 2048], osem)
        es3c.close()
        es3b.close()
        es3.close()

        n_o0, n_o1 = p.sem_n["outw0"], p.sem_n["outw1"]
        ndbg = p.sem_n["dbgw"]
        p.items["sync"].append(lambda e, v=n_o0: e.wait_ge(p.handles["outw0"], v))
        p.items["sync"].append(lambda e, v=n_o1: e.wait_ge(p.handles["outw1"], v))
        if ndbg:
            p.items["sync"].append(
                lambda e, v=ndbg: e.wait_ge(p.handles["dbgw"], v))

        with nc.Block() as block:
            @block.tensor
            def _(t):
                for fn in p.items["tensor"]:
                    fn(t)

            @block.vector
            def _(v):
                for fn in p.items["vector"]:
                    fn(v)

            @block.scalar
            def _(s):
                for fn in p.items["scalar"]:
                    fn(s)

            @block.sync
            def _(sy):
                for fn in p.items["sync"]:
                    fn(sy)

            @block.gpsimd
            def _(g):
                for fn in p.items["gpsimd"]:
                    fn(g)

    return nc


def _prep_shared(inputs):
    W_qkv = np.asarray(inputs["L0_attn_W_qkv"], dtype=np.float32)
    W_o = np.asarray(inputs["L0_attn_W_o"], dtype=np.float32)
    gate_w = np.asarray(inputs["L0_ffn_gate_w"], dtype=np.float32)
    sgw = np.asarray(inputs["L0_ffn_shared_gate_w"], dtype=np.float32)
    sguw = np.asarray(inputs["L0_ffn_shared_gate_up_w"], dtype=np.float32)
    sdw = np.asarray(inputs["L0_ffn_shared_down_w"], dtype=np.float32)
    w13 = np.asarray(inputs["L0_ffn_w13"], dtype=np.float32)
    w2 = np.asarray(inputs["L0_ffn_w2"], dtype=np.float32)
    wmod = np.asarray(inputs["L0_wmod"], dtype=np.float32)
    c = np.asarray(inputs["L0_c"], dtype=np.float32)

    d = {}
    d["wq"] = _tile_w(W_qkv)
    d["wo"] = _tile_w(W_o)
    g9 = np.concatenate([gate_w.T, sgw.T], axis=1)
    d["g9"] = np.ascontiguousarray(
        g9.reshape(16, 128, 9).transpose(1, 0, 2).reshape(128, 144),
        dtype=np.float32)
    d["sgu"] = _tile_w(np.ascontiguousarray(sguw.T))
    sdT = np.ascontiguousarray(sdw.T)
    d["sdn"] = np.ascontiguousarray(np.stack(
        [_tile_w(sdT[1408 * q:1408 * (q + 1), :]) for q in range(4)]))
    d["w13"] = np.ascontiguousarray(np.stack(
        [_tile_w(np.ascontiguousarray(w13[e].T)) for e in range(8)]))
    d["w2"] = np.ascontiguousarray(np.stack(
        [_tile_w(np.ascontiguousarray(w2[e].T)) for e in range(8)]))
    d["cin"] = np.ascontiguousarray(c.reshape(D))
    wmod_slices = []
    for core in range(NCORES):
        ws = wmod[:, 1536 * core:1536 * (core + 1)]
        t = ws.reshape(16, 128, 3, 512).transpose(2, 1, 0, 3).reshape(3, 128, 8192)
        wmod_slices.append(np.ascontiguousarray(t, dtype=np.float32))
    d["wmod_slices"] = wmod_slices
    d["rk4l"] = _host_rk4l()
    d["tri"] = _host_tri()
    d["iden16"] = np.eye(128, dtype=BF)
    d["iden32"] = np.eye(128, dtype=np.float32)
    d["ones128"] = np.ones((128, 1), dtype=BF)
    e0 = np.zeros((128, 128), dtype=BF)
    e0[:, 0] = BF(1.0)
    d["e0mat"] = e0
    d["onecol32"] = np.ones((1, 128), dtype=np.float32)
    d["rk4r"] = [_host_rk4r(core) for core in range(NCORES)]
    return d


def _run(inputs, dbg=0):
    x = np.asarray(inputs["L0_attn_x"], dtype=np.float32).reshape(S, D)
    d = _prep_shared(inputs)
    nc = build_nc(dbg=dbg)
    in_maps = []
    for core in range(NCORES):
        xc = np.ascontiguousarray(
            np.concatenate([x[CH * core:CH * (core + 1)],
                            x[CH * (15 - core):CH * (16 - core)]], axis=0))
        in_maps.append(
            {"xin": xc, "cin": d["cin"], "wmods": d["wmod_slices"][core],
             "wq": d["wq"], "wo": d["wo"], "g9": d["g9"], "sgu": d["sgu"],
             "sdn": d["sdn"], "w13": d["w13"], "w2": d["w2"],
             "rk4l": d["rk4l"], "rk4r": d["rk4r"][core], "tri": d["tri"],
             "iden16": d["iden16"], "iden32": d["iden32"],
             "ones128": d["ones128"], "onecol32": d["onecol32"],
             "e0mat": d["e0mat"]})
    res = run_bass_kernel_spmd(nc, in_maps, core_ids=list(range(NCORES)))
    out = np.empty((S, D), dtype=np.float32)
    for core in range(NCORES):
        yc = res.results[core]["yout"]
        out[CH * core:CH * (core + 1)] = yc[0:CH]
        out[CH * (15 - core):CH * (16 - core)] = yc[CH:2 * CH]
    return out.reshape(1, S, D), res


def kernel(**inputs):
    out, _ = _run(inputs, dbg=0)
    return out


# version 12
# speedup vs baseline: 1.0874x; 1.0874x over previous
import numpy as np
import ml_dtypes
from contextlib import ExitStack

import concourse.bass as bass
import concourse.mybir as mybir
from concourse.bass_utils import run_bass_kernel_spmd
from concourse.alu_op_type import AluOpType

F32 = mybir.dt.float32
BF16 = mybir.dt.bfloat16
AF = mybir.ActivationFunctionType
AX = mybir.AxisListType
BF = ml_dtypes.bfloat16

NCORES = 8
S, D, H, HD = 8192, 2048, 16, 128
CH = 512
KC = 16
TOW = 1024
B_OFF = 30.0
NEG = -1e30
WRING = 4
QSC = float(HD) ** -0.5

SLOPES = 2.0 ** (-8.0 / np.arange(1, H + 1, dtype=np.float64))


def _hilo(v):
    hi = v.astype(BF)
    lo = (v.astype(np.float32) - hi.astype(np.float32)).astype(BF)
    return hi, lo


def _tile_w(wt, Gm=128):
    K, M = wt.shape
    Kc, G = K // 128, M // Gm
    t = wt.reshape(Kc, 128, G, Gm).transpose(2, 1, 0, 3).reshape(G, 128, Kc * Gm)
    return np.ascontiguousarray(t, dtype=np.float32)


def _host_rk4r(core):
    chunks = [core, 15 - core]
    out = np.zeros((H, 128, 32 * 512), dtype=BF)
    qt = np.arange(CH, dtype=np.float64)
    for h in range(H):
        sl = SLOPES[h]
        for s in range(2):
            cs = chunks[s]
            i_abs = CH * cs + qt
            for it in range(16):
                j = s * 16 + it
                blk, base = j, 0
                col = blk * 512
                if it == 15:
                    v = sl * (CH * cs - i_abs) - B_OFF
                elif it < cs and (s == 1 or it <= 6):
                    v = sl * (CH * it - i_abs) - B_OFF
                else:
                    v = np.full(CH, NEG)
                hi, lo = _hilo(v.astype(np.float32))
                out[h, base + 0, col:col + 512] = hi
                out[h, base + 1, col:col + 512] = lo
                out[h, base + 2, col:col + 512] = BF(1.0)
                out[h, base + 3, col:col + 512] = BF(1.0)
    return out


def _host_rk4l():
    out = np.zeros((128, H * 4 * 128), dtype=BF)
    kt = np.arange(128, dtype=np.float64)
    for h in range(H):
        for sub in range(4):
            col = (h * 4 + sub) * 128
            hi, lo = _hilo((SLOPES[h] * (128 * sub + kt)).astype(np.float32))
            for base in (0,):
                out[base + 0, col:col + 128] = BF(1.0)
                out[base + 1, col:col + 128] = BF(1.0)
                out[base + 2, col:col + 128] = hi
                out[base + 3, col:col + 128] = lo
    return out


def _host_tri():
    tri = np.zeros((128, 4 * 512), dtype=np.float32)
    ktc = np.arange(128)[:, None]
    qtc = np.arange(512)[None, :]
    for sub in range(4):
        tri[:, sub * 512:(sub + 1) * 512] = np.where(
            128 * sub + ktc <= qtc, 0.0, NEG)
    return tri


def _own(k):
    return (k, 0) if k <= 7 else (15 - k, 1)


class Prog:
    def __init__(self, nc, es):
        self.nc = nc
        self.items = {k: [] for k in ("tensor", "vector", "scalar", "sync", "gpsimd")}
        self.sem_n = {}
        self.handles = {}
        self.last_wait = {}
        self.es = es

    def sem(self, name):
        self.handles[name] = self.es.enter_context(self.nc.semaphore(name))
        self.sem_n[name] = 0

    def wait(self, eng, sem, val):
        if val <= 0:
            return
        assert val <= self.sem_n[sem], f"wait {sem}>={val} > {self.sem_n[sem]}"
        key = (eng, sem)
        if self.last_wait.get(key, -1) >= val:
            return
        self.last_wait[key] = val
        h = self.handles[sem]
        self.items[eng].append(lambda e, h=h, v=val: e.wait_ge(h, v))

    def emit(self, eng, fn, inc=None, inck=1):
        if inc is not None:
            h = self.handles[inc]
            self.sem_n[inc] += inck
            self.items[eng].append(lambda e, fn=fn, h=h, k=inck: fn(e).then_inc(h, k))
            return self.sem_n[inc]
        self.items[eng].append(fn)
        return None

    def dma(self, q, out, in_, inc, transpose=False, slow=False):
        return self.emit(
            q, lambda e, o=out, i=in_, t=transpose, sl=slow: e.dma_start(
                out=o, in_=i, transpose=t, allow_slow_non_contiguous=sl),
            inc=inc, inck=16)


def build_nc(dbg=0):
    nc = bass.Bass("TRN2", target_bir_lowering=False)
    dt = nc.dram_tensor

    xin = dt("xin", [TOW, D], F32, kind="ExternalInput").ap()
    cin = dt("cin", [D], F32, kind="ExternalInput").ap()
    wmods = dt("wmods", [3, 128, 8192], F32, kind="ExternalInput").ap()
    wq_d = dt("wq", [48, 128, 2048], F32, kind="ExternalInput").ap()
    wo_d = dt("wo", [16, 128, 2048], F32, kind="ExternalInput").ap()
    g9_d = dt("g9", [128, 144], F32, kind="ExternalInput").ap()
    sgu_d = dt("sgu", [88, 128, 2048], F32, kind="ExternalInput").ap()
    sdn_d = dt("sdn", [4, 16, 128, 1408], F32, kind="ExternalInput").ap()
    w13_d = dt("w13", [8, 22, 128, 2048], F32, kind="ExternalInput").ap()
    w2_d = dt("w2", [8, 16, 128, 1408], F32, kind="ExternalInput").ap()
    rk4l_d = dt("rk4l", [128, H * 4 * 128], BF16, kind="ExternalInput").ap()
    rk4r_d = dt("rk4r", [H, 128, 32 * 512], BF16, kind="ExternalInput").ap()
    tri_d = dt("tri", [128, 4 * 512], F32, kind="ExternalInput").ap()
    iden16_d = dt("iden16", [128, 128], BF16, kind="ExternalInput").ap()
    iden32_d = dt("iden32", [128, 128], F32, kind="ExternalInput").ap()
    ones128_d = dt("ones128", [128, 1], BF16, kind="ExternalInput").ap()
    e0mat_d = dt("e0mat", [128, 128], BF16, kind="ExternalInput").ap()
    onecol32_d = dt("onecol32", [1, 128], F32, kind="ExternalInput").ap()
    yout = dt("yout", [TOW, D], F32, kind="ExternalOutput").ap()

    modsl = dt("modsl", [1536], F32)
    modfl = dt("modfl", [12288], F32)
    kvina = dt("kvina", [2048, TOW], BF16)
    kvinb = dt("kvinb", [2048, TOW], BF16)
    agkva = dt("agkva", [NCORES * 2048, TOW], BF16, addr_space="Shared")
    agkvb = dt("agkvb", [NCORES * 2048, TOW], BF16, addr_space="Shared")
    x1d = dt("x1d", [TOW, D], F32)
    wd9 = dt("wd9", [9, TOW], F32)

    if dbg >= 1:
        dbg_y = dt("dbg_y", [D, TOW], F32, kind="ExternalOutput").ap()
        dbg_x1 = dt("dbg_x1", [TOW, D], F32, kind="ExternalOutput").ap()
        dbg_w = dt("dbg_w", [9, TOW], F32, kind="ExternalOutput").ap()
    if dbg >= 2:
        dbg_h = dt("dbg_h", [D, TOW], F32, kind="ExternalOutput").ap()
        dbg_q = dt("dbg_q", [D, TOW], F32, kind="ExternalOutput").ap()
        dbg_kv = dt("dbg_kv", [4096, TOW], BF16, kind="ExternalOutput").ap()
        dbg_h2 = dt("dbg_h2", [D, TOW], F32, kind="ExternalOutput").ap()

    es = ExitStack()
    sb = lambda n, s, d: es.enter_context(nc.sbuf_tensor(n, s, d))
    pt = lambda n, s, d: es.enter_context(nc.psum_tensor(n, s, d))

    with es:
        wring = sb("wring", [128, WRING * 2048], BF16)
        modv = sb("modv", [128, 96], F32)
        ct = sb("ct", [128, 16], F32)
        iden16 = sb("iden16s", [128, 128], BF16)
        iden32 = sb("iden32s", [128, 128], F32)
        ones128 = sb("ones128s", [128, 1], BF16)
        e0mat = sb("e0mats", [128, 128], BF16)
        onecol32 = sb("onecol32s", [1, 128], F32)
        epscol = sb("epscol", [128, 1], F32)
        stat = sb("stat", [128, 16], F32)
        bnbuf = sb("bnbuf", [128, 48], F32)
        dbst = sb("dbst", [128, 1024], F32) if dbg else None

        pb0 = pt("pb0", [128, 512], F32)
        pb1 = pt("pb1", [128, 512], F32)
        pb2 = pt("pb2", [128, 512], F32)
        psc = pt("psc", [128, 2048], F32)
        pT = pt("pT", [128, 1024], BF16)
        obanks = [pb0, pb1]
        rot = [(psc[:, 512 * j:512 * (j + 1)], f"sc{j}") for j in range(4)]
        rot9 = [(psc[0:9, 512 * j:512 * (j + 1)], f"sc{j}") for j in range(4)]

        p = Prog(nc, es)
        for s_ in ["wl0", "wl1", "wl2", "wl3", "wml", "wc", "gmm",
                   "xl0", "xl1", "xl2", "xl3", "xl4", "xl5", "xl6", "xl7", "tbl",
                   "tbl2", "s0", "v0", "t0", "modw",
                   "cc1", "cc2", "cc3", "modr", "lnv", "lns", "lnt", "lne",
                   "kvw0", "kvw1", "kvw2", "kvw3",
                   "kvw4", "kvw5", "kvw6", "kvw7",
                   "gev", "sev", "vtm", "vte", "kld0", "kld1", "kld2", "kld3",
                   "vld0", "vld1", "vld2", "vld3", "rkl0", "rkl1",
                   "qk", "av", "ex", "pv",
                   "oev", "lcv", "bcm", "x1w0", "x1w1", "wd9w", "xr0", "xr1",
                   "wrl", "gtv",
                   "gts", "asv", "auv", "aev", "dev", "cmb", "fin",
                   "outw0", "outw1", "x1r0", "x1r1", "dbgw"]:
            p.sem(s_)

        bank_free = {}

        def bank_wait(eng, bname):
            if bname in bank_free:
                sem, val = bank_free[bname]
                p.wait(eng, sem, val)

        def bank_release(bname, sem):
            bank_free[bname] = (sem, p.sem_n[sem])

        p.dma("sync", iden16[:, :], iden16_d, "tbl")
        p.dma("sync", iden32[:, :], iden32_d, "tbl")
        p.dma("sync", ones128[:, :], ones128_d, "tbl")
        p.dma("sync", e0mat[:, :], e0mat_d, "tbl")
        p.dma("sync", onecol32[:, :], onecol32_d, "tbl")
        p.dma("sync", ct[:, :], cin.rearrange("(a b) -> b a", b=128), "tbl", slow=True)
        ntbl0 = p.sem_n["tbl"]
        p.wait("vector", "tbl", ntbl0)
        p.emit("vector", lambda v: v.memset(epscol[:, :], 1e-5), inc="v0")
        p.wait("scalar", "tbl", ntbl0)
        p.emit("scalar", lambda s: s.activation(ct[:, :], ct[:, :], AF.Silu), inc="s0")

        es_h1 = ExitStack()
        h_fm = es_h1.enter_context(nc.sbuf_tensor("h_fm", [128, KC, TOW], BF16, side="right"))
        es1 = ExitStack()
        sb1 = lambda n, s, d: es1.enter_context(nc.sbuf_tensor(n, s, d))
        wmr = sb1("wmr", [128, 8192], F32)
        modst = sb1("modst", [1, 1536], F32)
        x_sb = sb1("x_sb", [128, 8 * 2048], F32)
        z_sb = sb1("z_sb", [128, 2 * 2048], BF16)

        for cg in range(3):
            if cg > 0:
                p.wait("gpsimd", "t0", p.sem_n["t0"])
            p.dma("gpsimd", wmr[:, :], wmods[cg, :, :], "wml")
            p.wait("tensor", "wml", p.sem_n["wml"])
            p.wait("tensor", "s0", p.sem_n["s0"])
            bap, bname0 = rot[cg % 2]
            bank_wait("tensor", bname0)
            for k in range(KC):
                p.emit("tensor", lambda t, b=bap, k=k: t.matmul(
                    b[0:1, :], ct[:, k:k + 1], wmr[:, k * 512:(k + 1) * 512],
                    start=(k == 0), stop=(k == KC - 1)),
                    inc=("t0" if k == KC - 1 else None))
            p.wait("vector", "t0", p.sem_n["t0"])
            p.emit("vector", lambda v, b=bap, cg=cg: v.tensor_copy(
                modst[0:1, cg * 512:(cg + 1) * 512], b[0:1, :]), inc="v0")
            bank_release(bname0, "v0")
        p.wait("sync", "v0", p.sem_n["v0"])
        p.dma("sync", modsl.ap().rearrange("(a b) -> a b", a=1), modst[0:1, :], "modw")
        p.wait("gpsimd", "modw", p.sem_n["modw"])
        p.emit("gpsimd", lambda g: g.collective_compute(
            "AllGather", AluOpType.bypass, replica_groups=[list(range(NCORES))],
            ins=[modsl.ap().opt()], outs=[modfl.ap().opt()]), inc="cc1")
        p.wait("sync", "cc1", 1)
        for i in range(6):
            p.dma("sync", modv[:, 16 * i:16 * (i + 1)],
                  modfl.ap()[i * 2048:(i + 1) * 2048].rearrange("(a b) -> b a", b=128),
                  "modr", slow=True)
        nmodr = p.sem_n["modr"]
        p.wait("vector", "modr", nmodr)
        p.emit("vector", lambda v: v.tensor_scalar(
            modv[:, 16:32], modv[:, 16:32], 1.0, None, AluOpType.add), inc="v0")
        p.emit("vector", lambda v: v.tensor_scalar(
            modv[:, 64:80], modv[:, 64:80], 1.0, None, AluOpType.add), inc="v0")
        for base in (32, 80):
            p.wait("scalar", "modr", nmodr)
            p.wait("scalar", "s0", p.sem_n["s0"])
            p.wait("scalar", "v0", p.sem_n["v0"])
            p.emit("scalar", lambda s, b=base: s.activation(
                bnbuf[:, 0:16], modv[:, b:b + 16], AF.Abs), inc="s0")
            p.wait("vector", "s0", p.sem_n["s0"])
            p.emit("vector", lambda v: v.tensor_scalar(
                bnbuf[:, 0:16], bnbuf[:, 0:16], 0.25, None, AluOpType.max), inc="v0")
            p.wait("scalar", "v0", p.sem_n["v0"])
            p.emit("scalar", lambda s, b=base: s.activation(
                modv[:, b:b + 16], modv[:, b:b + 16], AF.Sign), inc="s0")
            p.wait("vector", "s0", p.sem_n["s0"])
            p.wait("vector", "v0", p.sem_n["v0"])
            p.emit("vector", lambda v, b=base: v.tensor_tensor(
                modv[:, b:b + 16], modv[:, b:b + 16], bnbuf[:, 0:16],
                AluOpType.mult), inc="v0")
        nmod_v = p.sem_n["v0"]

        for i in range(8):
            p.dma("sync", x_sb[:, i * 2048:(i + 1) * 2048],
                  xin[i * 128:(i + 1) * 128, :], f"xl{i}")

        def emit_ln(xt_ap, z_slice, hdst, gcol, scol, stat_waits, z_waits,
                    evac_waits, par=0):
            so = par * 8
            bo = par * 24
            for (sem, val) in stat_waits:
                p.wait("vector", sem, val)
            p.wait("vector", "lnv", p.sem_n["lnv"])
            p.wait("vector", "v0", p.sem_n["v0"])
            for j in range(4):
                p.emit("vector", lambda v, j=j: v.bn_stats(
                    bnbuf[:, bo + 6 * j:bo + 6 * (j + 1)],
                    xt_ap[:, 512 * j:512 * (j + 1)]), inc="lnv")
            p.wait("vector", "lnv", p.sem_n["lnv"])
            p.emit("vector", lambda v: v.bn_aggr(
                stat[:, so:so + 2], bnbuf[:, bo:bo + 24]), inc="lnv")
            p.wait("scalar", "lnv", p.sem_n["lnv"])
            p.emit("scalar", lambda s: s.activation(
                stat[:, so + 2:so + 3], stat[:, so + 1:so + 2], AF.Sqrt,
                bias=epscol[:, 0:1]), inc="lns")
            p.wait("vector", "lns", p.sem_n["lns"])
            p.emit("vector", lambda v: v.reciprocal(
                stat[:, so + 3:so + 4], stat[:, so + 2:so + 3]), inc="lnv")
            p.wait("vector", "lnv", p.sem_n["lnv"])
            p.emit("vector", lambda v: v.tensor_tensor(
                stat[:, so + 4:so + 5], stat[:, so:so + 1],
                stat[:, so + 3:so + 4], AluOpType.mult), inc="lnv")
            p.wait("vector", "lnv", p.sem_n["lnv"])
            p.emit("vector", lambda v: v.tensor_scalar(
                stat[:, so + 4:so + 5], stat[:, so + 4:so + 5], -1.0, None,
                AluOpType.mult), inc="lnv")
            p.wait("scalar", "lnv", p.sem_n["lnv"])
            for (sem, val) in stat_waits + z_waits:
                p.wait("scalar", sem, val)
            p.emit("scalar", lambda s: s.activation(
                z_slice, xt_ap, AF.Identity, bias=stat[:, so + 4:so + 5],
                scale=stat[:, so + 3:so + 4]), inc="lns")
            ns = p.sem_n["lns"]
            for half in range(2):
                p.wait("tensor", "lns", ns)
                bank_wait("tensor", "pT")
                for j in range(8):
                    kc = half * 8 + j
                    p.emit("tensor", lambda t, j=j, kc=kc: t.matmul(
                        pT[:, 128 * j:128 * (j + 1)],
                        z_slice[:, 128 * kc:128 * (kc + 1)], iden16[:, :],
                        start=(j == 0), stop=(j == 7), is_transpose=True),
                        inc=("lnt" if j == 7 else None))
                p.wait("vector", "lnt", p.sem_n["lnt"])
                p.wait("vector", "v0", nmod_v)
                for (sem, val) in evac_waits:
                    p.wait("vector", sem, val)
                for j in range(8):
                    kc = half * 8 + j
                    dst = hdst(kc)
                    p.emit("vector", lambda v, j=j, d=dst, kc=kc: v.tensor_scalar(
                        d, pT[:, 128 * j:128 * (j + 1)],
                        modv[:, gcol * 16 + kc:gcol * 16 + kc + 1],
                        modv[:, scol * 16 + kc:scol * 16 + kc + 1],
                        AluOpType.mult, AluOpType.add),
                        inc=("lne" if j == 7 else None))
                bank_release("pT", "lne")

        zfree = [("lnt", 0), ("lnt", 0)]
        for i in range(8):
            zsl = z_sb[:, (i % 2) * 2048:(i % 2 + 1) * 2048]
            emit_ln(x_sb[:, i * 2048:(i + 1) * 2048], zsl,
                    lambda kc, i=i: h_fm[:, kc, 128 * i:128 * (i + 1)],
                    1, 0, [(f"xl{i}", 16)], [zfree[i % 2]], [], par=i % 2)
            zfree[i % 2] = ("lnt", p.sem_n["lnt"])
        n_h = p.sem_n["lne"]
        es1.close()

        if dbg >= 2:
            for kc in range(KC):
                p.wait("vector", "lne", n_h)
                p.wait("vector", "dbgw", p.sem_n["dbgw"])
                p.emit("vector", lambda v, kc=kc: v.tensor_copy(
                    dbst[:, :], h_fm[:, kc, :]), inc="v0")
                p.wait("sync", "v0", p.sem_n["v0"])
                p.dma("sync", dbg_h[128 * kc:128 * (kc + 1), :], dbst[:, :], "dbgw")

        wg_state = {"i": 0}
        wc_of_group = {}

        def wload(src_ap, width):
            gi = wg_state["i"]
            wg_state["i"] += 1
            slot = gi % WRING
            wsem = f"wl{slot}"
            if gi >= WRING:
                p.wait("gpsimd", "wc", wc_of_group[gi - WRING])
            p.dma("gpsimd", wring[:, slot * 2048:slot * 2048 + width], src_ap, wsem)
            return gi, slot, (wsem, p.sem_n[wsem])

        def wdone(gi):
            wc_of_group[gi] = p.sem_n["wc"]

        def proj_group(src_ap, width, Kc, Gm, rhs_fn, banks, evac_fn, rhs_waits=(),
                       rhs_waits_tt=None):
            gi, slot, wlv = wload(src_ap, width)
            p.wait("tensor", wlv[0], wlv[1])
            for (sem, val) in rhs_waits:
                p.wait("tensor", sem, val)
            for tt in range(2):
                if rhs_waits_tt is not None:
                    for (sem, val) in rhs_waits_tt[tt]:
                        p.wait("tensor", sem, val)
                bnk, bname = banks[tt]
                bank_wait("tensor", bname)
                for k in range(Kc):
                    last = k == Kc - 1
                    wap = wring[:, slot * 2048 + k * Gm:slot * 2048 + (k + 1) * Gm]
                    rap = rhs_fn(k, tt)
                    p.emit("tensor", lambda t, b=bnk, w=wap, r=rap, st=(k == 0),
                           sp=last: t.matmul(b, w, r, start=st, stop=sp),
                           inc=("wc" if (last and tt == 1) else
                                ("gmm" if last else None)))
                if tt == 1:
                    wdone(gi)
                    gv = ("wc", p.sem_n["wc"])
                else:
                    gv = ("gmm", p.sem_n["gmm"])
                evac_fn(tt, bnk, bname, gv)

        es2 = ExitStack()
        sb2 = lambda n, s, d: es2.enter_context(nc.sbuf_tensor(n, s, d))
        y_fm = sb2("y_fm", [128, KC, TOW], BF16)
        es2a = ExitStack()
        sb2a = lambda n, s, d: es2a.enter_context(nc.sbuf_tensor(n, s, d))
        q_fm = sb2a("q_fm", [128, H, TOW], BF16)
        kvst = sb2a("kvst", [128, 4 * 512], BF16)
        kvst2 = sb2a("kvst2", [128, 2 * 512], BF16)
        kv_seq = {"i": 0}
        kvst_free = [None] * 4

        kvst2_free = [None, None]
        kvst2_i = {"i": 0}

        def emit_kv_group(fo, grp):
            def evac(tt, bnk, bname, gv, fo=fo, grp=grp):
                sem, val = gv
                n = kv_seq["i"]
                kv_seq["i"] += 1
                sl = n % 4
                ksem = f"kvw{sl + 4 * grp}"
                is_v = fo >= 32
                p.wait("scalar", sem, val)
                if kvst_free[sl] is not None:
                    p.wait("scalar", kvst_free[sl][0], kvst_free[sl][1])
                p.emit("scalar", lambda s, b=bnk, sl=sl: s.copy(
                    kvst[:, sl * 512:(sl + 1) * 512], b[:, :]), inc="sev")
                bank_release(bname, "sev")
                dst = kvina if grp == 0 else kvinb
                h9 = (fo - 16) % 8 if not is_v else (fo - 32) % 8
                if not is_v:
                    p.wait("sync", "sev", p.sem_n["sev"])
                    row = 128 * h9
                    p.dma("sync", dst.ap()[row:row + 128, tt * 512:(tt + 1) * 512],
                          kvst[:, sl * 512:(sl + 1) * 512], ksem)
                    kvst_free[sl] = (ksem, p.sem_n[ksem])
                else:
                    p.wait("tensor", "sev", p.sem_n["sev"])
                    bank_wait("tensor", "pT")
                    for sub in range(4):
                        p.emit("tensor", lambda t, sl=sl, sub=sub: t.matmul(
                            pT[:, 128 * sub:128 * (sub + 1)],
                            kvst[:, sl * 512 + 128 * sub:sl * 512 + 128 * (sub + 1)],
                            iden16[:, :], start=(sub == 0), stop=(sub == 3),
                            is_transpose=True), inc=("vtm" if sub == 3 else None))
                    kvst_free[sl] = ("vtm", p.sem_n["vtm"])
                    sl2 = kvst2_i["i"] % 2
                    kvst2_i["i"] += 1
                    p.wait("vector", "vtm", p.sem_n["vtm"])
                    if kvst2_free[sl2] is not None:
                        p.wait("vector", kvst2_free[sl2][0], kvst2_free[sl2][1])
                    p.emit("vector", lambda v, sl2=sl2: v.tensor_copy(
                        kvst2[:, sl2 * 512:(sl2 + 1) * 512], pT[:, 0:512]),
                        inc="vte")
                    bank_release("pT", "vte")
                    p.wait("sync", "vte", p.sem_n["vte"])
                    oap = dst.ap()[1024 + tt * 512:1024 + (tt + 1) * 512,
                                   128 * h9:128 * (h9 + 1)].rearrange(
                        "(a b) c -> b a c", b=128)
                    p.dma("sync", oap,
                          kvst2[:, sl2 * 512:(sl2 + 1) * 512].rearrange(
                              "p (a c) -> p a c", a=4), ksem)
                    kvst2_free[sl2] = (ksem, p.sem_n[ksem])

            bsel = [rot[(2 * fo + t) % 4] for t in range(2)]
            proj_group(wq_d[fo, :, :], 2048, KC, 128,
                       lambda k, tt: h_fm[:, k, tt * 512:(tt + 1) * 512],
                       bsel, evac, rhs_waits=[("lne", n_h)])

        def emit_q_group(fo):
            def evac(tt, bnk, bname, gv, fo=fo):
                sem, val = gv
                p.wait("vector", sem, val)
                p.emit("vector", lambda v, b=bnk, fo=fo, tt=tt: v.tensor_scalar(
                    q_fm[:, fo, tt * 512:(tt + 1) * 512], b[:, :], QSC, None,
                    AluOpType.mult), inc="gev")
                bank_release(bname, "gev")

            bsel = [rot[(2 * fo + t) % 4] for t in range(2)]
            proj_group(wq_d[fo, :, :], 2048, KC, 128,
                       lambda k, tt: h_fm[:, k, tt * 512:(tt + 1) * 512],
                       bsel, evac, rhs_waits=[("lne", n_h)])

        for fo in list(range(16, 24)) + list(range(32, 40)):
            emit_kv_group(fo, 0)
        kvw_tot = [0] * 8
        for i in range(4):
            kvw_tot[i] = p.sem_n[f"kvw{i}"]
            p.wait("gpsimd", f"kvw{i}", kvw_tot[i])
        p.emit("gpsimd", lambda g: g.collective_compute(
            "AllGather", AluOpType.bypass, replica_groups=[list(range(NCORES))],
            ins=[kvina.ap().opt()], outs=[agkva.ap().opt()]), inc="cc2")
        for fo in list(range(24, 32)) + list(range(40, 48)):
            emit_kv_group(fo, 1)
        for i in range(4):
            kvw_tot[4 + i] = p.sem_n[f"kvw{4 + i}"]
            p.wait("gpsimd", f"kvw{4 + i}", kvw_tot[4 + i])
        p.emit("gpsimd", lambda g: g.collective_compute(
            "AllGather", AluOpType.bypass, replica_groups=[list(range(NCORES))],
            ins=[kvinb.ap().opt()], outs=[agkvb.ap().opt()]), inc="cc3")
        for fo in range(16):
            emit_q_group(fo)
        n_q = p.sem_n["gev"]
        es_h1.close()

        if dbg >= 2:
            for kc in range(KC):
                p.wait("vector", "gev", n_q)
                p.wait("vector", "dbgw", p.sem_n["dbgw"])
                p.emit("vector", lambda v, kc=kc: v.tensor_copy(
                    dbst[:, :], q_fm[:, kc, :]), inc="v0")
                p.wait("sync", "v0", p.sem_n["v0"])
                p.dma("sync", dbg_q[128 * kc:128 * (kc + 1), :], dbst[:, :], "dbgw")
            for i in range(8):
                p.wait("sync", f"kvw{i}", kvw_tot[i])
            p.dma("sync", dbg_kv[0:2048, :], kvina.ap()[:, :], "dbgw")
            p.dma("sync", dbg_kv[2048:4096, :], kvinb.ap()[:, :], "dbgw")

        tri = sb2a("tri_s", [128, 4 * 512], F32)
        rk4l = sb2a("rk4l_s", [128, H * 4 * 128], BF16)
        kbuf = sb2a("kbuf", [128, 4 * 512], BF16)
        vbuf = sb2a("vbuf", [128, 4 * 512], BF16)
        rkb = sb2a("rkb", [128, 2 * 16384], BF16)
        ptb = sb2a("ptb", [128, 4 * 1024], BF16)
        lbuf = sb2a("lbuf", [1, 4 * 512], F32)
        lbc = sb2a("lbc", [128, 2 * 512], F32)
        p.dma("sync", tri[:, :], tri_d, "tbl2")
        p.dma("sync", rk4l[:, :], rk4l_d, "tbl2")
        ntbl1 = p.sem_n["tbl2"]

        kv_ld = {"i": 0}
        ptb_i = {"i": 0}
        sc_i = {"i": 0}
        kfree = [("qk", 0)] * 4
        vfree = [("pv", 0)] * 4
        ptfree = [("pv", 0)] * 4
        rkfree = [("qk", 0), ("qk", 0)]
        ofree = [("oev", 0), ("oev", 0)]
        lfree = [("lcv", 0)]
        lbuf_bc = [0]
        pvq = []
        PVDEPTH = 2
        lbc_last = [0, 0]

        for h in range(H):
            rslot = h % 2
            sm, vv = rkfree[rslot]
            p.wait("sync", sm, vv)
            rsem = f"rkl{rslot}"
            p.dma("sync", rkb[:, rslot * 16384:(rslot + 1) * 16384],
                  rk4r_d[h, :, :], rsem)
            rk_ready = (rsem, p.sem_n[rsem])

            for s in range(2):
                unit = 2 * h + s
                ob = unit % 2
                obank, obname = obanks[ob], f"pb{ob}"
                lbank, lbname = pb2, "pb2"
                nkv = (7 if s == 0 else 15)
                it_list = [("d", s)] + [("k", j) for j in range(nkv)]
                grp = 0 if h < 8 else 1
                h9 = h % 8
                ccsem = "cc2" if grp == 0 else "cc3"
                for (kind, j) in it_list:
                    kslot = kv_ld["i"] % 4
                    kv_ld["i"] += 1
                    if kind == "d":
                        src, cc_w = (kvina if grp == 0 else kvinb).ap(), False
                        krow, col = 128 * h9, s * 512
                        vrow = 1024 + s * 512
                        itidx = 15
                    else:
                        own, half = _own(j)
                        src, cc_w = (agkva if grp == 0 else agkvb).ap(), True
                        krow = 2048 * own + 128 * h9
                        vrow = 2048 * own + 1024 + half * 512
                        col = half * 512
                        itidx = j
                    sm, vv = kfree[kslot]
                    p.wait("sync", sm, vv)
                    if cc_w:
                        p.wait("sync", ccsem, 1)
                    else:
                        for q4 in range(4):
                            p.wait("sync", f"kvw{q4 + 4 * grp}", kvw_tot[q4 + 4 * grp])
                    ksm = f"kld{kslot}"
                    p.dma("sync", kbuf[:, kslot * 512:(kslot + 1) * 512],
                          src[krow:krow + 128, col:col + 512], ksm)
                    kv_val = (ksm, p.sem_n[ksm])
                    sm, vv = vfree[kslot]
                    p.wait("sync", sm, vv)
                    vap_in = src[vrow:vrow + 512,
                                 128 * h9:128 * (h9 + 1)].rearrange(
                        "(a b) c -> b a c", b=128)
                    p.dma("sync",
                          vbuf[:, kslot * 512:(kslot + 1) * 512].rearrange(
                              "p (a c) -> p a c", a=4),
                          vap_in, f"vld{kslot}")
                    vv_val = (f"vld{kslot}", p.sem_n[f"vld{kslot}"])

                    jsl = s * 16 + itidx
                    blk, rbase = jsl, 0
                    first = kind == "d"
                    last = kind == "k" and j == nkv - 1
                    for half2 in range(2):
                        scs = sc_i["i"] % 2
                        sc_i["i"] += 1
                        scap = psc[:, scs * 1024:(scs + 1) * 1024]
                        scn0, scn1 = f"sc{2 * scs}", f"sc{2 * scs + 1}"
                        p.wait("tensor", kv_val[0], kv_val[1])
                        p.wait("tensor", rk_ready[0], rk_ready[1])
                        p.wait("tensor", "gev", n_q)
                        p.wait("tensor", "tbl2", ntbl1)
                        bank_wait("tensor", scn0)
                        bank_wait("tensor", scn1)
                        qap = q_fm[:, h, s * 512:(s + 1) * 512]
                        for s2 in range(2):
                            sub = 2 * half2 + s2
                            sap = scap[:, s2 * 512:(s2 + 1) * 512]
                            kap = kbuf[:, kslot * 512 + sub * 128:
                                       kslot * 512 + (sub + 1) * 128]
                            p.emit("tensor", lambda t, b=sap, k=kap, q=qap: t.matmul(
                                b, k, q, start=True, stop=False))
                            lap = rk4l[:,
                                       (h * 4 + sub) * 128:(h * 4 + sub + 1) * 128]
                            rap = rkb[:,
                                      rslot * 16384 + blk * 512:
                                      rslot * 16384 + (blk + 1) * 512]
                            p.emit("tensor", lambda t, b=sap, l=lap, r=rap: t.matmul(
                                b, l, r, start=False, stop=True), inc="qk")
                        nqk = p.sem_n["qk"]
                        if first:
                            p.wait("vector", "qk", nqk)
                            p.wait("vector", "tbl2", ntbl1)
                            p.wait("vector", "av", p.sem_n["av"])
                            p.emit("vector", lambda v, b=scap, half2=half2:
                                   v.tensor_tensor(
                                       b, b, tri[:, half2 * 1024:(half2 + 1) * 1024],
                                       AluOpType.add), inc="av")
                        pslot = ptb_i["i"] % 4
                        ptb_i["i"] += 1
                        sm, vv = ptfree[pslot]
                        p.wait("scalar", sm, vv)
                        p.wait("scalar", "qk", nqk)
                        if first:
                            p.wait("scalar", "av", p.sem_n["av"])
                        p.emit("scalar", lambda sc, b=scap, ps=pslot: sc.activation(
                            ptb[:, ps * 1024:(ps + 1) * 1024], b, AF.Exp),
                            inc="ex")
                        bank_release(scn0, "ex")
                        bank_release(scn1, "ex")
                        nex = p.sem_n["ex"]

                        def pv_emit(obank=obank, lbank=lbank, kslot=kslot,
                                    half2=half2, pslot=pslot, first=first,
                                    last=last, nex=nex, vv_val=vv_val,
                                    owait=(ofree[ob] if (first and half2 == 0)
                                           else None),
                                    lwait=(lfree[0] if (first and half2 == 0)
                                           else None),
                                    rslot=rslot, nqk=nqk):
                            p.wait("tensor", "ex", nex)
                            p.wait("tensor", vv_val[0], vv_val[1])
                            if owait is not None:
                                p.wait("tensor", owait[0], owait[1])
                                p.wait("tensor", lwait[0], lwait[1])
                            for s2 in range(2):
                                sub = 2 * half2 + s2
                                st = first and sub == 0
                                sp = last and sub == 3
                                vap = vbuf[:, kslot * 512 + sub * 128:
                                           kslot * 512 + (sub + 1) * 128]
                                pap = ptb[:, pslot * 1024 + s2 * 512:
                                          pslot * 1024 + (s2 + 1) * 512]
                                p.emit("tensor", lambda t, o=obank, v=vap, pp=pap,
                                       st=st, sp=sp:
                                       t.matmul(o[:, :], v, pp, start=st, stop=sp))
                                p.emit("tensor", lambda t, lb=lbank, pp=pap,
                                       st=st, sp=sp:
                                       t.matmul(lb[:, :], e0mat[:, :], pp,
                                                start=st, stop=sp), inc="pv")
                            ptfree[pslot] = ("pv", p.sem_n["pv"])
                            if half2 == 1:
                                vfree[kslot] = ("pv", p.sem_n["pv"])

                        pvq.append(pv_emit)
                        if len(pvq) > PVDEPTH:
                            pvq.pop(0)()
                        if half2 == 1:
                            kfree[kslot] = ("qk", p.sem_n["qk"])
                            rkfree[rslot] = ("qk", p.sem_n["qk"])

                while pvq:
                    pvq.pop(0)()
                lc = 0
                p.wait("vector", "pv", p.sem_n["pv"])
                p.wait("vector", "bcm", lbuf_bc[0])
                p.wait("vector", "lcv", p.sem_n["lcv"])
                p.emit("vector", lambda v, lb=lbank, lc=lc: v.tensor_copy(
                    lbuf[0:1, lc:lc + 512], lb[0:1, :]), inc="lcv")
                p.wait("vector", "lcv", p.sem_n["lcv"])
                p.emit("vector", lambda v, lc=lc: v.reciprocal(
                    lbuf[0:1, lc + 512:lc + 1024], lbuf[0:1, lc:lc + 512]),
                    inc="lcv")
                p.wait("tensor", "lcv", p.sem_n["lcv"])
                p.emit("tensor", lambda t, lb=lbank, lc=lc: t.matmul(
                    lb[:, :], onecol32[0:1, :], lbuf[0:1, lc + 512:lc + 1024],
                    start=True, stop=True), inc="bcm")
                lbuf_bc[0] = p.sem_n["bcm"]
                p.wait("vector", "bcm", p.sem_n["bcm"])
                p.wait("vector", "oev", lbc_last[ob])
                p.emit("vector", lambda v, lb=lbank, ob=ob: v.tensor_copy(
                    lbc[:, ob * 512:(ob + 1) * 512], lb[:, :]), inc="lcv")
                lfree[0] = ("lcv", p.sem_n["lcv"])
                p.wait("vector", "lcv", p.sem_n["lcv"])
                p.emit("vector", lambda v, o=obank, h=h, s=s, ob=ob: v.tensor_tensor(
                    y_fm[:, h, s * 512:(s + 1) * 512], o[:, :],
                    lbc[:, ob * 512:(ob + 1) * 512], AluOpType.mult), inc="oev")
                ofree[ob] = ("oev", p.sem_n["oev"])
                lbc_last[ob] = p.sem_n["oev"]
        n_attn = p.sem_n["oev"]
        es2a.close()
        h2_fm = es.enter_context(nc.sbuf_tensor("h2_fm", [128, KC, TOW], BF16, side="right"))

        if dbg >= 1:
            for kc in range(KC):
                p.wait("vector", "oev", n_attn)
                p.wait("vector", "dbgw", p.sem_n["dbgw"])
                p.emit("vector", lambda v, kc=kc: v.tensor_copy(
                    dbst[:, :], y_fm[:, kc, :]), inc="v0")
                p.wait("sync", "v0", p.sem_n["v0"])
                p.dma("sync", dbg_y[128 * kc:128 * (kc + 1), :], dbst[:, :], "dbgw")

        es2b = ExitStack()
        sb2b = lambda n, s, d: es2b.enter_context(nc.sbuf_tensor(n, s, d))
        wo_tmp = sb2b("wo_tmp", [128, KC, TOW], BF16)
        xre = sb2b("xre", [128, 2 * 2048], F32)
        x1t = sb2b("x1t", [128, 2 * 2048], F32)
        z2_sb = sb2b("z2_sb", [128, 2 * 2048], BF16)

        for fo in range(16):
            def evac(tt, bnk, bname, gv, fo=fo):
                sem, val = gv
                p.wait("vector", sem, val)
                p.wait("vector", "v0", nmod_v)
                p.emit("vector", lambda v, b=bnk, fo=fo, tt=tt: v.tensor_scalar(
                    wo_tmp[:, fo, tt * 512:(tt + 1) * 512], b[:, :],
                    modv[:, 32 + fo:33 + fo], None, AluOpType.mult), inc="gev")
                bank_release(bname, "gev")

            bsel = [rot[(2 * fo + t) % 4] for t in range(2)]
            proj_group(wo_d[fo, :, :], 2048, KC, 128,
                       lambda k, tt: y_fm[:, k, tt * 512:(tt + 1) * 512],
                       bsel, evac, rhs_waits=[("oev", n_attn)])
        n_wo = p.sem_n["gev"]

        xre_free = [("lnv", 0), ("lnv", 0)]
        x1t_free = [[], []]
        z2free = [("lnt", 0), ("lnt", 0)]
        for i in range(8):
            ii = i % 2
            sm, vv = xre_free[ii]
            p.wait("sync", sm, vv)
            xsm2 = f"xr{ii}"
            p.dma("sync", xre[:, ii * 2048:(ii + 1) * 2048],
                  xin[i * 128:(i + 1) * 128, :], xsm2)
            nxr = (xsm2, p.sem_n[xsm2])
            for half in range(2):
                p.wait("tensor", "gev", n_wo)
                bank_wait("tensor", "pT")
                for jj in range(8):
                    kc = half * 8 + jj
                    p.emit("tensor", lambda t, j=jj, kc=kc, i=i: t.matmul(
                        pT[:, 128 * j:128 * (j + 1)],
                        wo_tmp[:, kc, 128 * i:128 * (i + 1)], iden16[:, :],
                        start=(j == 0), stop=(j == 7), is_transpose=True),
                        inc=("lnt" if jj == 7 else None))
                p.wait("vector", "lnt", p.sem_n["lnt"])
                p.wait("vector", nxr[0], nxr[1])
                for (sem, val) in x1t_free[ii]:
                    p.wait("vector", sem, val)
                p.emit("vector", lambda v, half=half, ii=ii: v.tensor_tensor(
                    x1t[:, ii * 2048 + half * 1024:ii * 2048 + (half + 1) * 1024],
                    pT[:, :],
                    xre[:, ii * 2048 + half * 1024:ii * 2048 + (half + 1) * 1024],
                    AluOpType.add), inc="lnv")
                bank_release("pT", "lnv")
            nadd = p.sem_n["lnv"]
            xre_free[ii] = ("lnv", nadd)
            xsem = f"x1w{ii}"
            p.wait("sync", "lnv", nadd)
            p.dma("sync", x1d.ap()[i * 128:(i + 1) * 128, :],
                  x1t[:, ii * 2048:(ii + 1) * 2048], xsem)
            if dbg >= 1:
                p.dma("sync", dbg_x1[i * 128:(i + 1) * 128, :],
                      x1t[:, ii * 2048:(ii + 1) * 2048], "dbgw")
            emit_ln(x1t[:, ii * 2048:(ii + 1) * 2048],
                    z2_sb[:, ii * 2048:(ii + 1) * 2048],
                    lambda kc, i=i: h2_fm[:, kc, 128 * i:128 * (i + 1)],
                    4, 3, [("lnv", nadd)], [z2free[ii]],
                    [("oev", n_attn)], par=ii)
            if i == 3:
                lne_half = p.sem_n["lne"]
            z2free[ii] = ("lnt", p.sem_n["lnt"])
            x1t_free[ii] = [(xsem, p.sem_n[xsem]), ("lns", p.sem_n["lns"])]
            if dbg >= 1:
                x1t_free[ii].append(("dbgw", p.sem_n["dbgw"]))
        n_h2 = p.sem_n["lne"]
        es2b.close()
        es2.close()

        if dbg >= 2:
            for kc in range(KC):
                p.wait("vector", "lne", n_h2)
                p.wait("vector", "dbgw", p.sem_n["dbgw"])
                p.emit("vector", lambda v, kc=kc: v.tensor_copy(
                    dbst[:, :], h2_fm[:, kc, :]), inc="v0")
                p.wait("sync", "v0", p.sem_n["v0"])
                p.dma("sync", dbg_h2[128 * kc:128 * (kc + 1), :], dbst[:, :], "dbgw")

        es3 = ExitStack()
        sb3 = lambda n, s, d: es3.enter_context(nc.sbuf_tensor(n, s, d))
        bcw = sb3("bcw", [128, 9 * 1024], BF16)
        es3a = ExitStack()
        sb3a = lambda n, s, d: es3a.enter_context(nc.sbuf_tensor(n, s, d))
        g_sb = sb3a("g_sb", [9, TOW], F32)
        gtile = sb3a("gtile", [128, 8, 9], F32)
        gw = sb3a("gw", [128, 8, 8], F32)
        gpr = sb3a("gpr", [128, 8, 8], F32)
        gpn = sb3a("gpn", [128, 8, 8], F32)
        wrow = sb3a("wrow", [1, 9 * TOW], F32)

        p.wait("vector", "lnv", p.sem_n["lnv"])

        def evac_g(tt, bnk, bname, gv):
            sem, val = gv
            p.wait("vector", sem, val)
            p.wait("vector", "gtv", p.sem_n["gtv"])
            p.emit("vector", lambda v, b=bnk, tt=tt: v.tensor_copy(
                g_sb[0:9, tt * 512:(tt + 1) * 512], b), inc="gtv")
            bank_release(bname, "gtv")

        proj_group(g9_d[:, :], 144, KC, 9,
                   lambda k, tt: h2_fm[:, k, tt * 512:(tt + 1) * 512],
                   [rot9[0], rot9[1]],
                   evac_g, rhs_waits_tt=([("lne", lne_half)], [("lne", n_h2)]))

        for half in range(2):
            p.wait("tensor", "gtv", p.sem_n["gtv"])
            bank_wait("tensor", "pb2")
            for jj in range(4):
                i = 4 * half + jj
                p.emit("tensor", lambda t, i=i, jj=jj: t.matmul(
                    pb2[:, 9 * jj:9 * (jj + 1)], g_sb[0:9, 128 * i:128 * (i + 1)],
                    iden32[0:9, 0:9], start=(jj == 0), stop=(jj == 3),
                    is_transpose=True), inc=("gmm" if jj == 3 else None))
            p.wait("vector", "gmm", p.sem_n["gmm"])
            p.wait("vector", "gtv", p.sem_n["gtv"])
            p.emit("vector", lambda v, half=half: v.tensor_copy(
                gtile[:, 4 * half:4 * (half + 1), :], pb2[:, 0:36]), inc="gtv")
            bank_release("pb2", "gtv")
        p.wait("scalar", "gtv", p.sem_n["gtv"])
        p.wait("scalar", "gts", p.sem_n["gts"])
        p.emit("scalar", lambda s: s.activation(
            gw[:, :, :], gtile[:, :, 0:8], AF.Exp), inc="gts")
        p.wait("vector", "gts", p.sem_n["gts"])
        p.emit("vector", lambda v: v.reduce_sum(
            bnbuf[:, 0:8], gw[:, :, :], axis=AX.X), inc="gtv")
        p.wait("vector", "gtv", p.sem_n["gtv"])
        p.emit("vector", lambda v: v.reciprocal(bnbuf[:, 8:16], bnbuf[:, 0:8]),
               inc="gtv")
        p.wait("vector", "gtv", p.sem_n["gtv"])
        rcp_b = None
        p.emit("vector", lambda v: v.tensor_tensor(
            gpr[:, :, :], gw[:, :, :],
            bnbuf[:, 8:16].rearrange("p (a b) -> p a b", b=1).broadcast_to((128, 8, 8)),
            AluOpType.mult), inc="gtv")
        p.wait("vector", "gtv", p.sem_n["gtv"])
        p.emit("vector", lambda v: v.reduce_max(
            bnbuf[:, 16:24], gpr[:, :, :], axis=AX.X), inc="gtv")
        p.wait("vector", "gtv", p.sem_n["gtv"])
        p.emit("vector", lambda v: v.tensor_tensor(
            gpn[:, :, :], gpr[:, :, :],
            bnbuf[:, 16:24].rearrange("p (a b) -> p a b", b=1).broadcast_to((128, 8, 8)),
            AluOpType.is_ge), inc="gtv")
        p.wait("vector", "gtv", p.sem_n["gtv"])
        p.emit("vector", lambda v: v.tensor_scalar(
            gpn[:, :, :], gpn[:, :, :], NEG, None, AluOpType.mult), inc="gtv")
        p.wait("vector", "gtv", p.sem_n["gtv"])
        p.emit("vector", lambda v: v.tensor_tensor(
            gpn[:, :, :], gpn[:, :, :], gpr[:, :, :], AluOpType.add), inc="gtv")
        p.wait("vector", "gtv", p.sem_n["gtv"])
        p.emit("vector", lambda v: v.reduce_max(
            bnbuf[:, 24:32], gpn[:, :, :], axis=AX.X), inc="gtv")
        p.wait("vector", "gtv", p.sem_n["gtv"])
        p.emit("vector", lambda v: v.tensor_tensor(
            gpn[:, :, :], gpr[:, :, :],
            bnbuf[:, 24:32].rearrange("p (a b) -> p a b", b=1).broadcast_to((128, 8, 8)),
            AluOpType.is_ge), inc="gtv")
        p.wait("vector", "gtv", p.sem_n["gtv"])
        p.emit("vector", lambda v: v.tensor_tensor(
            gtile[:, :, 0:8], gpr[:, :, :], gpn[:, :, :], AluOpType.mult),
            inc="gtv")
        p.wait("scalar", "gtv", p.sem_n["gtv"])
        p.wait("scalar", "gts", p.sem_n["gts"])
        p.emit("scalar", lambda s: s.activation(
            gtile[:, :, 8:9], gtile[:, :, 8:9], AF.Sigmoid), inc="gts")
        for half in range(2):
            p.wait("tensor", "gtv", p.sem_n["gtv"])
            p.wait("tensor", "gts", p.sem_n["gts"])
            bank_wait("tensor", "pb2")
            for jj in range(4):
                i = 4 * half + jj
                p.emit("tensor", lambda t, i=i, jj=jj: t.matmul(
                    pb2[0:9, 128 * jj:128 * (jj + 1)], gtile[:, i, :],
                    iden32[:, :], start=(jj == 0), stop=(jj == 3),
                    is_transpose=True), inc=("gmm" if jj == 3 else None))
            p.wait("vector", "gmm", p.sem_n["gmm"])
            p.wait("vector", "gtv", p.sem_n["gtv"])
            p.emit("vector", lambda v, half=half: v.tensor_copy(
                g_sb[0:9, 512 * half:512 * (half + 1)], pb2[0:9, 0:512]),
                inc="gtv")
            bank_release("pb2", "gtv")
        p.wait("sync", "gtv", p.sem_n["gtv"])
        p.dma("sync", wd9.ap()[:, :], g_sb[0:9, :], "wd9w")
        p.wait("sync", "wd9w", p.sem_n["wd9w"])
        p.dma("sync", wrow[0:1, :], wd9.ap().rearrange("a b -> (a b)").rearrange("(a b) -> a b", a=1), "wrl")
        nwrow = p.sem_n["wrl"]
        if dbg >= 1:
            p.dma("sync", dbg_w[:, :], wd9.ap()[:, :], "dbgw")
        for r in range(9):
            for tt in range(2):
                p.wait("tensor", "wrl", nwrow)
                bank_wait("tensor", "pb2")
                p.emit("tensor", lambda t, r=r, tt=tt: t.matmul(
                    pb2[:, :], onecol32[0:1, :],
                    wrow[0:1, r * 1024 + tt * 512:r * 1024 + (tt + 1) * 512],
                    start=True, stop=True), inc="gmm")
                p.wait("vector", "gmm", p.sem_n["gmm"])
                p.wait("vector", "gtv", p.sem_n["gtv"])
                p.emit("vector", lambda v, r=r, tt=tt: v.tensor_copy(
                    bcw[:, r * 1024 + tt * 512:r * 1024 + (tt + 1) * 512],
                    pb2[:, :]), inc="gtv")
                bank_release("pb2", "gtv")
        n_bcw = p.sem_n["gtv"]
        es3a.close()

        es3b = ExitStack()
        sb3b = lambda n, s, d: es3b.enter_context(nc.sbuf_tensor(n, s, d))
        comb = sb3b("comb", [128, KC * TOW], F32)
        es3b1 = ExitStack()
        sb3b1 = lambda n, s, d: es3b1.enter_context(nc.sbuf_tensor(n, s, d))
        A_p = [sb3b1(f"A{jj}", [128, 11, TOW], BF16) for jj in range(2)]
        gtmp = sb3b1("gtmp", [128, 1024], BF16)
        utmp = sb3b1("utmp", [128, 1024], BF16)

        p.wait("vector", "lne", n_h2)
        p.wait("vector", "cmb", p.sem_n["cmb"])
        p.wait("vector", "x1w0", p.sem_n["x1w0"])
        p.wait("vector", "x1w1", p.sem_n["x1w1"])
        if dbg >= 1:
            p.wait("vector", "dbgw", p.sem_n["dbgw"])
        for kc in range(KC):
            p.emit("vector", lambda v, kc=kc: v.memset(
                comb[:, kc * 1024:(kc + 1) * 1024], 0.0), inc="cmb")
        ncmb0 = p.sem_n["cmb"]

        pieces = [("s", q) for q in range(4)] + [("e", e) for e in range(8)]
        a_last_use = [("wc", 0), ("wc", 0)]

        def build_A(pi, kind, idx):
            Ab = A_p[pi % 2]
            wcol = 8 if kind == "s" else idx
            asm, asv = a_last_use[pi % 2]
            for jj in range(11):
                if kind == "s":
                    gsrc = sgu_d[11 * idx + jj, :, :]
                    usrc = sgu_d[44 + 11 * idx + jj, :, :]
                else:
                    gsrc = w13_d[idx, jj, :, :]
                    usrc = w13_d[idx, 11 + jj, :, :]

                def evg(tt, bnk, bname, gv):
                    sem, val = gv
                    p.wait("scalar", sem, val)
                    p.wait("scalar", "aev", p.sem_n["aev"])
                    p.wait("scalar", "asv", p.sem_n["asv"])
                    p.emit("scalar", lambda s, b=bnk, tt=tt: s.activation(
                        gtmp[:, tt * 512:(tt + 1) * 512], b[:, :], AF.Silu),
                        inc="asv")
                    bank_release(bname, "asv")

                def evu(tt, bnk, bname, gv, wcol=wcol):
                    sem, val = gv
                    p.wait("vector", sem, val)
                    p.wait("vector", "gtv", n_bcw)
                    p.wait("vector", "aev", p.sem_n["aev"])
                    p.wait("vector", "auv", p.sem_n["auv"])
                    p.emit("vector", lambda v, b=bnk, tt=tt, w=wcol: v.tensor_tensor(
                        utmp[:, tt * 512:(tt + 1) * 512], b[:, :],
                        bcw[:, w * 1024 + tt * 512:w * 1024 + (tt + 1) * 512],
                        AluOpType.mult), inc="auv")
                    bank_release(bname, "auv")

                bsel0 = [rot[(2 * jj + t) % 4] for t in range(2)]
                bsel1 = [rot[(2 * jj + 2 + t) % 4] for t in range(2)]
                rw = [(asm, asv)] if jj == 0 else []
                rwtt = ([("lne", lne_half)], [("lne", n_h2)])
                proj_group(gsrc, 2048, KC, 128,
                           lambda k, tt: h2_fm[:, k, tt * 512:(tt + 1) * 512],
                           bsel0, evg, rhs_waits=rw, rhs_waits_tt=rwtt)
                proj_group(usrc, 2048, KC, 128,
                           lambda k, tt: h2_fm[:, k, tt * 512:(tt + 1) * 512],
                           bsel1, evu, rhs_waits_tt=rwtt)
                p.wait("vector", "asv", p.sem_n["asv"])
                p.wait("vector", "auv", p.sem_n["auv"])
                p.wait("vector", asm, asv)
                p.wait("vector", "aev", p.sem_n["aev"])
                p.emit("vector", lambda v, Ab=Ab, jj=jj: v.tensor_tensor(
                    Ab[:, jj, :], gtmp[:, :], utmp[:, :], AluOpType.mult),
                    inc="aev")

        def down_pair(pi0, pc0, pi1, pc1):
            nae = p.sem_n["aev"]
            for fo in range(16):
                for half, (pi, (kind, idx)) in enumerate([(pi0, pc0), (pi1, pc1)]):
                    src = sdn_d[idx, fo, :, :] if kind == "s" else w2_d[idx, fo, :, :]
                    gi, slot, wlv = wload(src, 1408)
                    p.wait("tensor", wlv[0], wlv[1])
                    if half == 0 and fo == 0:
                        p.wait("tensor", "aev", nae)
                    Ab = A_p[pi % 2]
                    for tt in range(2):
                        bnk, bname = obanks[tt], f"pb{tt}"
                        if half == 0:
                            bank_wait("tensor", bname)
                        for k in range(11):
                            wap = wring[:, slot * 2048 + k * 128:
                                        slot * 2048 + (k + 1) * 128]
                            rap = Ab[:, k, tt * 512:(tt + 1) * 512]
                            p.emit("tensor", lambda t, b=bnk, w=wap, r=rap,
                                   st=(half == 0 and k == 0),
                                   sp=(half == 1 and k == 10):
                                   t.matmul(b[:, :], w, r, start=st, stop=sp),
                                   inc=("wc" if (k == 10 and tt == 1) else None))
                    wdone(gi)
                nwc = p.sem_n["wc"]
                for tt in range(2):
                    bnk, bname = obanks[tt], f"pb{tt}"
                    p.wait("vector", "wc", nwc)
                    p.wait("vector", "cmb", ncmb0)
                    p.wait("vector", "dev", p.sem_n["dev"])
                    p.wait("vector", "v0", nmod_v)
                    cap = comb[:, fo * 1024 + tt * 512:fo * 1024 + (tt + 1) * 512]
                    p.emit("vector", lambda v, b=bnk, c=cap, fo=fo:
                           v.scalar_tensor_tensor(
                               c, b[:, :], modv[:, 80 + fo:81 + fo], c,
                               AluOpType.mult, AluOpType.add), inc="dev")
                    bank_release(bname, "dev")
            a_last_use[pi0 % 2] = ("wc", p.sem_n["wc"])
            a_last_use[pi1 % 2] = ("wc", p.sem_n["wc"])

        for pp in range(6):
            build_A(2 * pp, *pieces[2 * pp])
            build_A(2 * pp + 1, *pieces[2 * pp + 1])
            down_pair(2 * pp, pieces[2 * pp], 2 * pp + 1, pieces[2 * pp + 1])
        n_dev = p.sem_n["dev"]
        es3b1.close()

        es3c = ExitStack()
        sb3c = lambda n, s, d: es3c.enter_context(nc.sbuf_tensor(n, s, d))
        x1re = sb3c("x1re", [128, 2 * 2048], F32)
        outt = sb3c("outt", [128, 2 * 2048], F32)
        x1re_free = [("fin", 0), ("fin", 0)]
        n_x1w = [p.sem_n["x1w0"], p.sem_n["x1w1"]]
        for i in range(8):
            ii = i % 2
            sm, vv = x1re_free[ii]
            p.wait("sync", sm, vv)
            p.wait("sync", "x1w0", n_x1w[0])
            p.wait("sync", "x1w1", n_x1w[1])
            p.wait("sync", "dev", n_dev)
            qsm = f"x1r{ii}"
            p.dma("sync", x1re[:, ii * 2048:(ii + 1) * 2048],
                  x1d.ap()[i * 128:(i + 1) * 128, :], qsm)
            nx1r = (qsm, p.sem_n[qsm])
            osem = f"outw{ii}"
            for quad in range(4):
                bnk0, bname = rot[quad]
                bnk = bnk0
                p.wait("tensor", "dev", n_dev)
                bank_wait("tensor", bname)
                for jj in range(4):
                    kc = quad * 4 + jj
                    p.emit("tensor", lambda t, b=bnk, j=jj, kc=kc, i=i: t.matmul(
                        b[:, 128 * j:128 * (j + 1)],
                        comb[:, kc * 1024 + 128 * i:kc * 1024 + 128 * (i + 1)],
                        iden32[:, :], start=(j == 0), stop=(j == 3),
                        is_transpose=True), inc=("lnt" if jj == 3 else None))
                p.wait("vector", "lnt", p.sem_n["lnt"])
                p.wait("vector", nx1r[0], nx1r[1])
                p.wait("vector", osem, p.sem_n[osem])
                p.emit("vector", lambda v, b=bnk, quad=quad, ii=ii: v.tensor_tensor(
                    outt[:, ii * 2048 + quad * 512:ii * 2048 + (quad + 1) * 512],
                    b[:, :],
                    x1re[:, ii * 2048 + quad * 512:ii * 2048 + (quad + 1) * 512],
                    AluOpType.add), inc="fin")
                bank_release(bname, "fin")
            x1re_free[ii] = ("fin", p.sem_n["fin"])
            p.wait("sync", "fin", p.sem_n["fin"])
            p.dma("sync", yout[i * 128:(i + 1) * 128, :],
                  outt[:, ii * 2048:(ii + 1) * 2048], osem)
        es3c.close()
        es3b.close()
        es3.close()

        n_o0, n_o1 = p.sem_n["outw0"], p.sem_n["outw1"]
        ndbg = p.sem_n["dbgw"]
        p.items["sync"].append(lambda e, v=n_o0: e.wait_ge(p.handles["outw0"], v))
        p.items["sync"].append(lambda e, v=n_o1: e.wait_ge(p.handles["outw1"], v))
        if ndbg:
            p.items["sync"].append(
                lambda e, v=ndbg: e.wait_ge(p.handles["dbgw"], v))

        with nc.Block() as block:
            @block.tensor
            def _(t):
                for fn in p.items["tensor"]:
                    fn(t)

            @block.vector
            def _(v):
                for fn in p.items["vector"]:
                    fn(v)

            @block.scalar
            def _(s):
                for fn in p.items["scalar"]:
                    fn(s)

            @block.sync
            def _(sy):
                for fn in p.items["sync"]:
                    fn(sy)

            @block.gpsimd
            def _(g):
                for fn in p.items["gpsimd"]:
                    fn(g)

    return nc


def _prep_shared(inputs):
    W_qkv = np.asarray(inputs["L0_attn_W_qkv"], dtype=np.float32)
    W_o = np.asarray(inputs["L0_attn_W_o"], dtype=np.float32)
    gate_w = np.asarray(inputs["L0_ffn_gate_w"], dtype=np.float32)
    sgw = np.asarray(inputs["L0_ffn_shared_gate_w"], dtype=np.float32)
    sguw = np.asarray(inputs["L0_ffn_shared_gate_up_w"], dtype=np.float32)
    sdw = np.asarray(inputs["L0_ffn_shared_down_w"], dtype=np.float32)
    w13 = np.asarray(inputs["L0_ffn_w13"], dtype=np.float32)
    w2 = np.asarray(inputs["L0_ffn_w2"], dtype=np.float32)
    wmod = np.asarray(inputs["L0_wmod"], dtype=np.float32)
    c = np.asarray(inputs["L0_c"], dtype=np.float32)

    d = {}
    d["wq"] = _tile_w(W_qkv)
    d["wo"] = _tile_w(W_o)
    g9 = np.concatenate([gate_w.T, sgw.T], axis=1)
    d["g9"] = np.ascontiguousarray(
        g9.reshape(16, 128, 9).transpose(1, 0, 2).reshape(128, 144),
        dtype=np.float32)
    d["sgu"] = _tile_w(np.ascontiguousarray(sguw.T))
    sdT = np.ascontiguousarray(sdw.T)
    d["sdn"] = np.ascontiguousarray(np.stack(
        [_tile_w(sdT[1408 * q:1408 * (q + 1), :]) for q in range(4)]))
    d["w13"] = np.ascontiguousarray(np.stack(
        [_tile_w(np.ascontiguousarray(w13[e].T)) for e in range(8)]))
    d["w2"] = np.ascontiguousarray(np.stack(
        [_tile_w(np.ascontiguousarray(w2[e].T)) for e in range(8)]))
    d["cin"] = np.ascontiguousarray(c.reshape(D))
    wmod_slices = []
    for core in range(NCORES):
        ws = wmod[:, 1536 * core:1536 * (core + 1)]
        t = ws.reshape(16, 128, 3, 512).transpose(2, 1, 0, 3).reshape(3, 128, 8192)
        wmod_slices.append(np.ascontiguousarray(t, dtype=np.float32))
    d["wmod_slices"] = wmod_slices
    d["rk4l"] = _host_rk4l()
    d["tri"] = _host_tri()
    d["iden16"] = np.eye(128, dtype=BF)
    d["iden32"] = np.eye(128, dtype=np.float32)
    d["ones128"] = np.ones((128, 1), dtype=BF)
    e0 = np.zeros((128, 128), dtype=BF)
    e0[:, 0] = BF(1.0)
    d["e0mat"] = e0
    d["onecol32"] = np.ones((1, 128), dtype=np.float32)
    d["rk4r"] = [_host_rk4r(core) for core in range(NCORES)]
    return d


def _run(inputs, dbg=0):
    x = np.asarray(inputs["L0_attn_x"], dtype=np.float32).reshape(S, D)
    d = _prep_shared(inputs)
    nc = build_nc(dbg=dbg)
    in_maps = []
    for core in range(NCORES):
        xc = np.ascontiguousarray(
            np.concatenate([x[CH * core:CH * (core + 1)],
                            x[CH * (15 - core):CH * (16 - core)]], axis=0))
        in_maps.append(
            {"xin": xc, "cin": d["cin"], "wmods": d["wmod_slices"][core],
             "wq": d["wq"], "wo": d["wo"], "g9": d["g9"], "sgu": d["sgu"],
             "sdn": d["sdn"], "w13": d["w13"], "w2": d["w2"],
             "rk4l": d["rk4l"], "rk4r": d["rk4r"][core], "tri": d["tri"],
             "iden16": d["iden16"], "iden32": d["iden32"],
             "ones128": d["ones128"], "onecol32": d["onecol32"],
             "e0mat": d["e0mat"]})
    res = run_bass_kernel_spmd(nc, in_maps, core_ids=list(range(NCORES)))
    out = np.empty((S, D), dtype=np.float32)
    for core in range(NCORES):
        yc = res.results[core]["yout"]
        out[CH * core:CH * (core + 1)] = yc[0:CH]
        out[CH * (15 - core):CH * (16 - core)] = yc[CH:2 * CH]
    return out.reshape(1, S, D), res


def kernel(**inputs):
    out, _ = _run(inputs, dbg=0)
    return out


# version 13
# speedup vs baseline: 1.1043x; 1.0155x over previous
import numpy as np
import ml_dtypes
from contextlib import ExitStack

import concourse.bass as bass
import concourse.mybir as mybir
from concourse.bass_utils import run_bass_kernel_spmd
from concourse.alu_op_type import AluOpType

F32 = mybir.dt.float32
BF16 = mybir.dt.bfloat16
AF = mybir.ActivationFunctionType
AX = mybir.AxisListType
BF = ml_dtypes.bfloat16

NCORES = 8
S, D, H, HD = 8192, 2048, 16, 128
CH = 512
KC = 16
TOW = 1024
B_OFF = 30.0
NEG = -1e30
WRING = 4
QSC = float(HD) ** -0.5

SLOPES = 2.0 ** (-8.0 / np.arange(1, H + 1, dtype=np.float64))


def _hilo(v):
    hi = v.astype(BF)
    lo = (v.astype(np.float32) - hi.astype(np.float32)).astype(BF)
    return hi, lo


def _tile_w(wt, Gm=128):
    K, M = wt.shape
    Kc, G = K // 128, M // Gm
    t = wt.reshape(Kc, 128, G, Gm).transpose(2, 1, 0, 3).reshape(G, 128, Kc * Gm)
    return np.ascontiguousarray(t, dtype=np.float32)


def _host_rk4r(core):
    chunks = [core, 15 - core]
    out = np.zeros((H, 128, 32 * 512), dtype=BF)
    qt = np.arange(CH, dtype=np.float64)
    for h in range(H):
        sl = SLOPES[h]
        for s in range(2):
            cs = chunks[s]
            i_abs = CH * cs + qt
            for it in range(16):
                j = s * 16 + it
                blk, base = j, 0
                col = blk * 512
                if it == 15:
                    v = sl * (CH * cs - i_abs) - B_OFF
                elif it < cs and (s == 1 or it <= 6):
                    v = sl * (CH * it - i_abs) - B_OFF
                else:
                    v = np.full(CH, NEG)
                hi, lo = _hilo(v.astype(np.float32))
                out[h, base + 0, col:col + 512] = hi
                out[h, base + 1, col:col + 512] = lo
                out[h, base + 2, col:col + 512] = BF(1.0)
                out[h, base + 3, col:col + 512] = BF(1.0)
    return out


def _host_rk4l():
    out = np.zeros((128, H * 4 * 128), dtype=BF)
    kt = np.arange(128, dtype=np.float64)
    for h in range(H):
        for sub in range(4):
            col = (h * 4 + sub) * 128
            hi, lo = _hilo((SLOPES[h] * (128 * sub + kt)).astype(np.float32))
            for base in (0,):
                out[base + 0, col:col + 128] = BF(1.0)
                out[base + 1, col:col + 128] = BF(1.0)
                out[base + 2, col:col + 128] = hi
                out[base + 3, col:col + 128] = lo
    return out


def _host_tri():
    tri = np.zeros((128, 4 * 512), dtype=np.float32)
    ktc = np.arange(128)[:, None]
    qtc = np.arange(512)[None, :]
    for sub in range(4):
        tri[:, sub * 512:(sub + 1) * 512] = np.where(
            128 * sub + ktc <= qtc, 0.0, NEG)
    return tri


def _own(k):
    return (k, 0) if k <= 7 else (15 - k, 1)


class Prog:
    def __init__(self, nc, es):
        self.nc = nc
        self.items = {k: [] for k in ("tensor", "vector", "scalar", "sync", "gpsimd")}
        self.sem_n = {}
        self.handles = {}
        self.last_wait = {}
        self.es = es

    def sem(self, name):
        self.handles[name] = self.es.enter_context(self.nc.semaphore(name))
        self.sem_n[name] = 0

    def wait(self, eng, sem, val):
        if val <= 0:
            return
        assert val <= self.sem_n[sem], f"wait {sem}>={val} > {self.sem_n[sem]}"
        key = (eng, sem)
        if self.last_wait.get(key, -1) >= val:
            return
        self.last_wait[key] = val
        h = self.handles[sem]
        self.items[eng].append(lambda e, h=h, v=val: e.wait_ge(h, v))

    def emit(self, eng, fn, inc=None, inck=1):
        if inc is not None:
            h = self.handles[inc]
            self.sem_n[inc] += inck
            self.items[eng].append(lambda e, fn=fn, h=h, k=inck: fn(e).then_inc(h, k))
            return self.sem_n[inc]
        self.items[eng].append(fn)
        return None

    def dma(self, q, out, in_, inc, transpose=False, slow=False):
        return self.emit(
            q, lambda e, o=out, i=in_, t=transpose, sl=slow: e.dma_start(
                out=o, in_=i, transpose=t, allow_slow_non_contiguous=sl),
            inc=inc, inck=16)


def build_nc(dbg=0):
    nc = bass.Bass("TRN2", target_bir_lowering=False)
    dt = nc.dram_tensor

    xin = dt("xin", [TOW, D], F32, kind="ExternalInput").ap()
    cin = dt("cin", [D], F32, kind="ExternalInput").ap()
    wmods = dt("wmods", [3, 128, 8192], F32, kind="ExternalInput").ap()
    wq_d = dt("wq", [48, 128, 2048], F32, kind="ExternalInput").ap()
    wo_d = dt("wo", [16, 128, 2048], F32, kind="ExternalInput").ap()
    g9_d = dt("g9", [128, 144], F32, kind="ExternalInput").ap()
    sgu_d = dt("sgu", [88, 128, 2048], F32, kind="ExternalInput").ap()
    sdn_d = dt("sdn", [4, 16, 128, 1408], F32, kind="ExternalInput").ap()
    w13_d = dt("w13", [8, 22, 128, 2048], F32, kind="ExternalInput").ap()
    w2_d = dt("w2", [8, 16, 128, 1408], F32, kind="ExternalInput").ap()
    rk4l_d = dt("rk4l", [128, H * 4 * 128], BF16, kind="ExternalInput").ap()
    rk4r_d = dt("rk4r", [H, 128, 32 * 512], BF16, kind="ExternalInput").ap()
    tri_d = dt("tri", [128, 4 * 512], F32, kind="ExternalInput").ap()
    iden16_d = dt("iden16", [128, 128], BF16, kind="ExternalInput").ap()
    iden32_d = dt("iden32", [128, 128], F32, kind="ExternalInput").ap()
    ones128_d = dt("ones128", [128, 1], BF16, kind="ExternalInput").ap()
    e0mat_d = dt("e0mat", [128, 128], BF16, kind="ExternalInput").ap()
    onecol32_d = dt("onecol32", [1, 128], F32, kind="ExternalInput").ap()
    yout = dt("yout", [TOW, D], F32, kind="ExternalOutput").ap()

    modsl = dt("modsl", [1536], F32)
    modfl = dt("modfl", [12288], F32)
    kvina = dt("kvina", [2048, TOW], BF16)
    kvinb = dt("kvinb", [2048, TOW], BF16)
    agkva = dt("agkva", [NCORES * 2048, TOW], BF16, addr_space="Shared")
    agkvb = dt("agkvb", [NCORES * 2048, TOW], BF16, addr_space="Shared")
    x1d = dt("x1d", [TOW, D], F32)
    wd9 = dt("wd9", [9, TOW], F32)

    if dbg >= 1:
        dbg_y = dt("dbg_y", [D, TOW], F32, kind="ExternalOutput").ap()
        dbg_x1 = dt("dbg_x1", [TOW, D], F32, kind="ExternalOutput").ap()
        dbg_w = dt("dbg_w", [9, TOW], F32, kind="ExternalOutput").ap()
    if dbg >= 2:
        dbg_h = dt("dbg_h", [D, TOW], F32, kind="ExternalOutput").ap()
        dbg_q = dt("dbg_q", [D, TOW], F32, kind="ExternalOutput").ap()
        dbg_kv = dt("dbg_kv", [4096, TOW], BF16, kind="ExternalOutput").ap()
        dbg_h2 = dt("dbg_h2", [D, TOW], F32, kind="ExternalOutput").ap()

    es = ExitStack()
    sb = lambda n, s, d: es.enter_context(nc.sbuf_tensor(n, s, d))
    pt = lambda n, s, d: es.enter_context(nc.psum_tensor(n, s, d))

    with es:
        wring = sb("wring", [128, WRING * 2048], BF16)
        modv = sb("modv", [128, 96], F32)
        ct = sb("ct", [128, 16], F32)
        iden16 = sb("iden16s", [128, 128], BF16)
        iden32 = sb("iden32s", [128, 128], F32)
        ones128 = sb("ones128s", [128, 1], BF16)
        e0mat = sb("e0mats", [128, 128], BF16)
        onecol32 = sb("onecol32s", [1, 128], F32)
        epscol = sb("epscol", [128, 1], F32)
        stat = sb("stat", [128, 16], F32)
        bnbuf = sb("bnbuf", [128, 48], F32)
        dbst = sb("dbst", [128, 1024], F32) if dbg else None

        pb0 = pt("pb0", [128, 512], F32)
        pb1 = pt("pb1", [128, 512], F32)
        pb2 = pt("pb2", [128, 512], F32)
        psc = pt("psc", [128, 2048], F32)
        pT = pt("pT", [128, 1024], BF16)
        obanks = [pb0, pb1]
        rot = [(psc[:, 512 * j:512 * (j + 1)], f"sc{j}") for j in range(4)]
        rot9 = [(psc[0:9, 512 * j:512 * (j + 1)], f"sc{j}") for j in range(4)]

        p = Prog(nc, es)
        for s_ in ["wl0", "wl1", "wl2", "wl3", "wml", "wc", "gmm",
                   "xl0", "xl1", "xl2", "xl3", "xl4", "xl5", "xl6", "xl7", "tbl",
                   "tbl2", "s0", "v0", "t0", "modw",
                   "cc1", "cc2", "cc3", "modr", "lnv", "lns", "lnt", "lne",
                   "kvw0", "kvw1", "kvw2", "kvw3",
                   "kvw4", "kvw5", "kvw6", "kvw7",
                   "gev", "sev", "vtm", "vte", "kld0", "kld1", "kld2", "kld3",
                   "vld0", "vld1", "vld2", "vld3", "rkl0", "rkl1",
                   "qk", "av", "ex", "pv",
                   "oev", "lcv", "bcm", "x1w0", "x1w1", "wd9w", "xr0", "xr1",
                   "wrl", "gtv",
                   "gts", "asv", "auv", "aev", "dev", "cmb", "fin",
                   "outw0", "outw1", "x1r0", "x1r1", "dbgw"]:
            p.sem(s_)

        bank_free = {}

        def bank_wait(eng, bname):
            if bname in bank_free:
                sem, val = bank_free[bname]
                p.wait(eng, sem, val)

        def bank_release(bname, sem):
            bank_free[bname] = (sem, p.sem_n[sem])

        pass
        p.dma("sync", iden16[:, :], iden16_d, "tbl")
        p.dma("sync", iden32[:, :], iden32_d, "tbl")
        p.dma("sync", ones128[:, :], ones128_d, "tbl")
        p.dma("sync", e0mat[:, :], e0mat_d, "tbl")
        p.dma("sync", onecol32[:, :], onecol32_d, "tbl")
        p.dma("sync", ct[:, :], cin.rearrange("(a b) -> b a", b=128), "tbl", slow=True)
        ntbl0 = p.sem_n["tbl"]
        p.wait("vector", "tbl", ntbl0)
        p.emit("vector", lambda v: v.memset(epscol[:, :], 1e-5), inc="v0")
        p.wait("scalar", "tbl", ntbl0)
        p.emit("scalar", lambda s: s.activation(ct[:, :], ct[:, :], AF.Silu), inc="s0")

        es_h1 = ExitStack()
        h_fm = es_h1.enter_context(nc.sbuf_tensor("h_fm", [128, KC, TOW], BF16, side="right"))
        es1 = ExitStack()
        sb1 = lambda n, s, d: es1.enter_context(nc.sbuf_tensor(n, s, d))
        wmr = sb1("wmr", [128, 8192], F32)
        modst = sb1("modst", [1, 1536], F32)
        x_sb = sb1("x_sb", [128, 8 * 2048], F32)
        z_sb = sb1("z_sb", [128, 2 * 2048], F32)

        for cg in range(3):
            if cg > 0:
                p.wait("gpsimd", "t0", p.sem_n["t0"])
            p.dma("gpsimd", wmr[:, :], wmods[cg, :, :], "wml")
            p.wait("tensor", "wml", p.sem_n["wml"])
            p.wait("tensor", "s0", p.sem_n["s0"])
            bap, bname0 = rot[cg % 2]
            bank_wait("tensor", bname0)
            for k in range(KC):
                p.emit("tensor", lambda t, b=bap, k=k: t.matmul(
                    b[0:1, :], ct[:, k:k + 1], wmr[:, k * 512:(k + 1) * 512],
                    start=(k == 0), stop=(k == KC - 1)),
                    inc=("t0" if k == KC - 1 else None))
            p.wait("vector", "t0", p.sem_n["t0"])
            p.emit("vector", lambda v, b=bap, cg=cg: v.tensor_copy(
                modst[0:1, cg * 512:(cg + 1) * 512], b[0:1, :]), inc="v0")
            bank_release(bname0, "v0")
        p.wait("sync", "v0", p.sem_n["v0"])
        p.dma("sync", modsl.ap().rearrange("(a b) -> a b", a=1), modst[0:1, :], "modw")
        p.wait("gpsimd", "modw", p.sem_n["modw"])
        p.emit("gpsimd", lambda g: g.collective_compute(
            "AllGather", AluOpType.bypass, replica_groups=[list(range(NCORES))],
            ins=[modsl.ap().opt()], outs=[modfl.ap().opt()]), inc="cc1")
        p.wait("sync", "cc1", 1)
        for i in range(6):
            p.dma("sync", modv[:, 16 * i:16 * (i + 1)],
                  modfl.ap()[i * 2048:(i + 1) * 2048].rearrange("(a b) -> b a", b=128),
                  "modr", slow=True)
        nmodr = p.sem_n["modr"]
        p.wait("vector", "modr", nmodr)
        p.emit("vector", lambda v: v.tensor_scalar(
            modv[:, 16:32], modv[:, 16:32], 1.0, None, AluOpType.add), inc="v0")
        p.emit("vector", lambda v: v.tensor_scalar(
            modv[:, 64:80], modv[:, 64:80], 1.0, None, AluOpType.add), inc="v0")
        for base in (32, 80):
            p.wait("scalar", "modr", nmodr)
            p.wait("scalar", "s0", p.sem_n["s0"])
            p.wait("scalar", "v0", p.sem_n["v0"])
            p.emit("scalar", lambda s, b=base: s.activation(
                bnbuf[:, 0:16], modv[:, b:b + 16], AF.Abs), inc="s0")
            p.wait("vector", "s0", p.sem_n["s0"])
            p.emit("vector", lambda v: v.tensor_scalar(
                bnbuf[:, 0:16], bnbuf[:, 0:16], 0.25, None, AluOpType.max), inc="v0")
            p.wait("scalar", "v0", p.sem_n["v0"])
            p.emit("scalar", lambda s, b=base: s.activation(
                modv[:, b:b + 16], modv[:, b:b + 16], AF.Sign), inc="s0")
            p.wait("vector", "s0", p.sem_n["s0"])
            p.wait("vector", "v0", p.sem_n["v0"])
            p.emit("vector", lambda v, b=base: v.tensor_tensor(
                modv[:, b:b + 16], modv[:, b:b + 16], bnbuf[:, 0:16],
                AluOpType.mult), inc="v0")
        nmod_v = p.sem_n["v0"]

        for i in range(8):
            p.dma("sync", x_sb[:, i * 2048:(i + 1) * 2048],
                  xin[i * 128:(i + 1) * 128, :], f"xl{i}")

        def emit_ln(xt_ap, z_slice, hdst, gcol, scol, stat_waits, z_waits,
                    evac_waits, par=0):
            so = par * 8
            bo = par * 24
            for (sem, val) in stat_waits:
                p.wait("vector", sem, val)
            p.wait("vector", "lnv", p.sem_n["lnv"])
            p.wait("vector", "v0", p.sem_n["v0"])
            for j in range(4):
                p.emit("vector", lambda v, j=j: v.bn_stats(
                    bnbuf[:, bo + 6 * j:bo + 6 * (j + 1)],
                    xt_ap[:, 512 * j:512 * (j + 1)]), inc="lnv")
            p.wait("vector", "lnv", p.sem_n["lnv"])
            p.emit("vector", lambda v: v.bn_aggr(
                stat[:, so:so + 2], bnbuf[:, bo:bo + 24]), inc="lnv")
            p.wait("scalar", "lnv", p.sem_n["lnv"])
            p.emit("scalar", lambda s: s.activation(
                stat[:, so + 2:so + 3], stat[:, so + 1:so + 2], AF.Sqrt,
                bias=epscol[:, 0:1]), inc="lns")
            p.wait("vector", "lns", p.sem_n["lns"])
            p.emit("vector", lambda v: v.reciprocal(
                stat[:, so + 3:so + 4], stat[:, so + 2:so + 3]), inc="lnv")
            p.wait("vector", "lnv", p.sem_n["lnv"])
            p.emit("vector", lambda v: v.tensor_tensor(
                stat[:, so + 4:so + 5], stat[:, so:so + 1],
                stat[:, so + 3:so + 4], AluOpType.mult), inc="lnv")
            p.wait("vector", "lnv", p.sem_n["lnv"])
            p.emit("vector", lambda v: v.tensor_scalar(
                stat[:, so + 4:so + 5], stat[:, so + 4:so + 5], -1.0, None,
                AluOpType.mult), inc="lnv")
            p.wait("scalar", "lnv", p.sem_n["lnv"])
            for (sem, val) in stat_waits + z_waits:
                p.wait("scalar", sem, val)
            p.emit("scalar", lambda s: s.activation(
                z_slice, xt_ap, AF.Identity, bias=stat[:, so + 4:so + 5],
                scale=stat[:, so + 3:so + 4]), inc="lns")
            ns = p.sem_n["lns"]
            for quad in range(4):
                bnk, bname = rot[quad]
                p.wait("tensor", "lns", ns)
                bank_wait("tensor", bname)
                for j in range(4):
                    kc = quad * 4 + j
                    p.emit("tensor", lambda t, b=bnk, j=j, kc=kc: t.matmul(
                        b[:, 128 * j:128 * (j + 1)],
                        z_slice[:, 128 * kc:128 * (kc + 1)], iden32[:, :],
                        start=(j == 0), stop=(j == 3), is_transpose=True),
                        inc=("lnt" if j == 3 else None))
                p.wait("vector", "lnt", p.sem_n["lnt"])
                p.wait("vector", "v0", nmod_v)
                for (sem, val) in evac_waits:
                    p.wait("vector", sem, val)
                for j in range(4):
                    kc = quad * 4 + j
                    dst = hdst(kc)
                    p.emit("vector", lambda v, b=bnk, j=j, d=dst, kc=kc:
                           v.tensor_scalar(
                               d, b[:, 128 * j:128 * (j + 1)],
                               modv[:, gcol * 16 + kc:gcol * 16 + kc + 1],
                               modv[:, scol * 16 + kc:scol * 16 + kc + 1],
                               AluOpType.mult, AluOpType.add),
                           inc=("lne" if j == 3 else None))
                bank_release(bname, "lne")

        zfree = [("lnt", 0), ("lnt", 0)]
        lne_h1 = 0
        for i in range(8):
            zsl = z_sb[:, (i % 2) * 2048:(i % 2 + 1) * 2048]
            emit_ln(x_sb[:, i * 2048:(i + 1) * 2048], zsl,
                    lambda kc, i=i: h_fm[:, kc, 128 * i:128 * (i + 1)],
                    1, 0, [(f"xl{i}", 16)], [zfree[i % 2]], [], par=i % 2)
            zfree[i % 2] = ("lnt", p.sem_n["lnt"])
            if i == 3:
                lne_h1 = p.sem_n["lne"]
        n_h = p.sem_n["lne"]
        es1.close()

        if dbg >= 2:
            for kc in range(KC):
                p.wait("vector", "lne", n_h)
                p.wait("vector", "dbgw", p.sem_n["dbgw"])
                p.emit("vector", lambda v, kc=kc: v.tensor_copy(
                    dbst[:, :], h_fm[:, kc, :]), inc="v0")
                p.wait("sync", "v0", p.sem_n["v0"])
                p.dma("sync", dbg_h[128 * kc:128 * (kc + 1), :], dbst[:, :], "dbgw")

        wg_state = {"i": 0}
        wc_of_group = {}

        def wload(src_ap, width):
            gi = wg_state["i"]
            wg_state["i"] += 1
            slot = gi % WRING
            wsem = f"wl{slot}"
            if gi >= WRING:
                p.wait("gpsimd", "wc", wc_of_group[gi - WRING])
            p.dma("gpsimd", wring[:, slot * 2048:slot * 2048 + width], src_ap, wsem)
            return gi, slot, (wsem, p.sem_n[wsem])

        def wdone(gi):
            wc_of_group[gi] = p.sem_n["wc"]

        def proj_group(src_ap, width, Kc, Gm, rhs_fn, banks, evac_fn, rhs_waits=(),
                       rhs_waits_tt=None):
            gi, slot, wlv = wload(src_ap, width)
            p.wait("tensor", wlv[0], wlv[1])
            for (sem, val) in rhs_waits:
                p.wait("tensor", sem, val)
            for tt in range(2):
                if rhs_waits_tt is not None:
                    for (sem, val) in rhs_waits_tt[tt]:
                        p.wait("tensor", sem, val)
                bnk, bname = banks[tt]
                bank_wait("tensor", bname)
                for k in range(Kc):
                    last = k == Kc - 1
                    wap = wring[:, slot * 2048 + k * Gm:slot * 2048 + (k + 1) * Gm]
                    rap = rhs_fn(k, tt)
                    p.emit("tensor", lambda t, b=bnk, w=wap, r=rap, st=(k == 0),
                           sp=last: t.matmul(b, w, r, start=st, stop=sp),
                           inc=("wc" if (last and tt == 1) else
                                ("gmm" if last else None)))
                if tt == 1:
                    wdone(gi)
                    gv = ("wc", p.sem_n["wc"])
                else:
                    gv = ("gmm", p.sem_n["gmm"])
                evac_fn(tt, bnk, bname, gv)

        es2 = ExitStack()
        sb2 = lambda n, s, d: es2.enter_context(nc.sbuf_tensor(n, s, d))
        y_fm = sb2("y_fm", [128, KC, TOW], BF16)
        es2a = ExitStack()
        sb2a = lambda n, s, d: es2a.enter_context(nc.sbuf_tensor(n, s, d))
        q_fm = sb2a("q_fm", [128, H, TOW], BF16)
        kvst = sb2a("kvst", [128, 4 * 512], BF16)
        kvst2 = sb2a("kvst2", [128, 2 * 512], BF16)
        kv_seq = {"i": 0}
        kvst_free = [None] * 4

        kvst2_free = [None, None]
        kvst2_i = {"i": 0}

        def emit_kv_group(fo, grp):
            def evac(tt, bnk, bname, gv, fo=fo, grp=grp):
                sem, val = gv
                n = kv_seq["i"]
                kv_seq["i"] += 1
                sl = n % 4
                ksem = f"kvw{sl + 4 * grp}"
                is_v = fo >= 32
                p.wait("scalar", sem, val)
                if kvst_free[sl] is not None:
                    p.wait("scalar", kvst_free[sl][0], kvst_free[sl][1])
                p.emit("scalar", lambda s, b=bnk, sl=sl: s.copy(
                    kvst[:, sl * 512:(sl + 1) * 512], b[:, :]), inc="sev")
                bank_release(bname, "sev")
                dst = kvina if grp == 0 else kvinb
                h9 = (fo - 16) % 8 if not is_v else (fo - 32) % 8
                if not is_v:
                    p.wait("sync", "sev", p.sem_n["sev"])
                    row = 128 * h9
                    p.dma("sync", dst.ap()[row:row + 128, tt * 512:(tt + 1) * 512],
                          kvst[:, sl * 512:(sl + 1) * 512], ksem)
                    kvst_free[sl] = (ksem, p.sem_n[ksem])
                else:
                    p.wait("tensor", "sev", p.sem_n["sev"])
                    bank_wait("tensor", "pT")
                    for sub in range(4):
                        p.emit("tensor", lambda t, sl=sl, sub=sub: t.matmul(
                            pT[:, 128 * sub:128 * (sub + 1)],
                            kvst[:, sl * 512 + 128 * sub:sl * 512 + 128 * (sub + 1)],
                            iden16[:, :], start=(sub == 0), stop=(sub == 3),
                            is_transpose=True), inc=("vtm" if sub == 3 else None))
                    kvst_free[sl] = ("vtm", p.sem_n["vtm"])
                    sl2 = kvst2_i["i"] % 2
                    kvst2_i["i"] += 1
                    p.wait("vector", "vtm", p.sem_n["vtm"])
                    if kvst2_free[sl2] is not None:
                        p.wait("vector", kvst2_free[sl2][0], kvst2_free[sl2][1])
                    p.emit("vector", lambda v, sl2=sl2: v.tensor_copy(
                        kvst2[:, sl2 * 512:(sl2 + 1) * 512], pT[:, 0:512]),
                        inc="vte")
                    bank_release("pT", "vte")
                    p.wait("sync", "vte", p.sem_n["vte"])
                    oap = dst.ap()[1024 + tt * 512:1024 + (tt + 1) * 512,
                                   128 * h9:128 * (h9 + 1)].rearrange(
                        "(a b) c -> b a c", b=128)
                    p.dma("sync", oap,
                          kvst2[:, sl2 * 512:(sl2 + 1) * 512].rearrange(
                              "p (a c) -> p a c", a=4), ksem)
                    kvst2_free[sl2] = (ksem, p.sem_n[ksem])

            bsel = [rot[(2 * fo + t) % 4] for t in range(2)]
            proj_group(wq_d[fo, :, :], 2048, KC, 128,
                       lambda k, tt: h_fm[:, k, tt * 512:(tt + 1) * 512],
                       bsel, evac,
                       rhs_waits_tt=([("lne", lne_h1)], [("lne", n_h)]))

        def emit_q_group(fo):
            def evac(tt, bnk, bname, gv, fo=fo):
                sem, val = gv
                p.wait("vector", sem, val)
                p.emit("vector", lambda v, b=bnk, fo=fo, tt=tt: v.tensor_scalar(
                    q_fm[:, fo, tt * 512:(tt + 1) * 512], b[:, :], QSC, None,
                    AluOpType.mult), inc="gev")
                bank_release(bname, "gev")

            bsel = [rot[(2 * fo + t) % 4] for t in range(2)]
            proj_group(wq_d[fo, :, :], 2048, KC, 128,
                       lambda k, tt: h_fm[:, k, tt * 512:(tt + 1) * 512],
                       bsel, evac,
                       rhs_waits_tt=([("lne", lne_h1)], [("lne", n_h)]))

        for fo in list(range(16, 24)) + list(range(32, 40)):
            emit_kv_group(fo, 0)
        kvw_tot = [0] * 8
        for i in range(4):
            kvw_tot[i] = p.sem_n[f"kvw{i}"]
            p.wait("gpsimd", f"kvw{i}", kvw_tot[i])
        p.emit("gpsimd", lambda g: g.collective_compute(
            "AllGather", AluOpType.bypass, replica_groups=[list(range(NCORES))],
            ins=[kvina.ap().opt()], outs=[agkva.ap().opt()]), inc="cc2")
        for fo in list(range(24, 32)) + list(range(40, 48)):
            emit_kv_group(fo, 1)
        for i in range(4):
            kvw_tot[4 + i] = p.sem_n[f"kvw{4 + i}"]
            p.wait("gpsimd", f"kvw{4 + i}", kvw_tot[4 + i])
        p.emit("gpsimd", lambda g: g.collective_compute(
            "AllGather", AluOpType.bypass, replica_groups=[list(range(NCORES))],
            ins=[kvinb.ap().opt()], outs=[agkvb.ap().opt()]), inc="cc3")
        for fo in range(16):
            emit_q_group(fo)
        n_q = p.sem_n["gev"]
        es_h1.close()

        if dbg >= 2:
            for kc in range(KC):
                p.wait("vector", "gev", n_q)
                p.wait("vector", "dbgw", p.sem_n["dbgw"])
                p.emit("vector", lambda v, kc=kc: v.tensor_copy(
                    dbst[:, :], q_fm[:, kc, :]), inc="v0")
                p.wait("sync", "v0", p.sem_n["v0"])
                p.dma("sync", dbg_q[128 * kc:128 * (kc + 1), :], dbst[:, :], "dbgw")
            for i in range(8):
                p.wait("sync", f"kvw{i}", kvw_tot[i])
            p.dma("sync", dbg_kv[0:2048, :], kvina.ap()[:, :], "dbgw")
            p.dma("sync", dbg_kv[2048:4096, :], kvinb.ap()[:, :], "dbgw")

        tri = sb2a("tri_s", [128, 4 * 512], F32)
        rk4l = sb2a("rk4l_s", [128, H * 4 * 128], BF16)
        kbuf = sb2a("kbuf", [128, 4 * 512], BF16)
        vbuf = sb2a("vbuf", [128, 4 * 512], BF16)
        rkb = sb2a("rkb", [128, 2 * 16384], BF16)
        ptb = sb2a("ptb", [128, 4 * 1024], BF16)
        lbuf = sb2a("lbuf", [1, 4 * 512], F32)
        lbc = sb2a("lbc", [128, 2 * 512], F32)
        p.dma("sync", tri[:, :], tri_d, "tbl2")
        p.dma("sync", rk4l[:, :], rk4l_d, "tbl2")
        ntbl1 = p.sem_n["tbl2"]

        kv_ld = {"i": 0}
        ptb_i = {"i": 0}
        sc_i = {"i": 0}
        kfree = [("qk", 0)] * 4
        vfree = [("pv", 0)] * 4
        ptfree = [("pv", 0)] * 4
        rkfree = [("qk", 0), ("qk", 0)]
        ofree = [("oev", 0), ("oev", 0)]
        lfree = [("lcv", 0)]
        lbuf_bc = [0]
        pvq = []
        PVDEPTH = 2
        lbc_last = [0, 0]

        for h in range(H):
            rslot = h % 2
            sm, vv = rkfree[rslot]
            p.wait("sync", sm, vv)
            rsem = f"rkl{rslot}"
            p.dma("sync", rkb[:, rslot * 16384:(rslot + 1) * 16384],
                  rk4r_d[h, :, :], rsem)
            rk_ready = (rsem, p.sem_n[rsem])

            for s in range(2):
                unit = 2 * h + s
                ob = unit % 2
                obank, obname = obanks[ob], f"pb{ob}"
                lbank, lbname = pb2, "pb2"
                nkv = (7 if s == 0 else 15)
                it_list = [("d", s)] + [("k", j) for j in range(nkv)]
                grp = 0 if h < 8 else 1
                h9 = h % 8
                ccsem = "cc2" if grp == 0 else "cc3"
                for (kind, j) in it_list:
                    kslot = kv_ld["i"] % 4
                    kv_ld["i"] += 1
                    if kind == "d":
                        src, cc_w = (kvina if grp == 0 else kvinb).ap(), False
                        krow, col = 128 * h9, s * 512
                        vrow = 1024 + s * 512
                        itidx = 15
                    else:
                        own, half = _own(j)
                        src, cc_w = (agkva if grp == 0 else agkvb).ap(), True
                        krow = 2048 * own + 128 * h9
                        vrow = 2048 * own + 1024 + half * 512
                        col = half * 512
                        itidx = j
                    sm, vv = kfree[kslot]
                    p.wait("sync", sm, vv)
                    if cc_w:
                        p.wait("sync", ccsem, 1)
                    else:
                        for q4 in range(4):
                            p.wait("sync", f"kvw{q4 + 4 * grp}", kvw_tot[q4 + 4 * grp])
                    ksm = f"kld{kslot}"
                    p.dma("sync", kbuf[:, kslot * 512:(kslot + 1) * 512],
                          src[krow:krow + 128, col:col + 512], ksm)
                    kv_val = (ksm, p.sem_n[ksm])
                    sm, vv = vfree[kslot]
                    p.wait("sync", sm, vv)
                    vap_in = src[vrow:vrow + 512,
                                 128 * h9:128 * (h9 + 1)].rearrange(
                        "(a b) c -> b a c", b=128)
                    p.dma("sync",
                          vbuf[:, kslot * 512:(kslot + 1) * 512].rearrange(
                              "p (a c) -> p a c", a=4),
                          vap_in, f"vld{kslot}")
                    vv_val = (f"vld{kslot}", p.sem_n[f"vld{kslot}"])

                    jsl = s * 16 + itidx
                    blk, rbase = jsl, 0
                    first = kind == "d"
                    last = kind == "k" and j == nkv - 1
                    for half2 in range(2):
                        scs = sc_i["i"] % 2
                        sc_i["i"] += 1
                        scap = psc[:, scs * 1024:(scs + 1) * 1024]
                        scn0, scn1 = f"sc{2 * scs}", f"sc{2 * scs + 1}"
                        p.wait("tensor", kv_val[0], kv_val[1])
                        p.wait("tensor", rk_ready[0], rk_ready[1])
                        p.wait("tensor", "gev", n_q)
                        p.wait("tensor", "tbl2", ntbl1)
                        bank_wait("tensor", scn0)
                        bank_wait("tensor", scn1)
                        qap = q_fm[:, h, s * 512:(s + 1) * 512]
                        for s2 in range(2):
                            sub = 2 * half2 + s2
                            sap = scap[:, s2 * 512:(s2 + 1) * 512]
                            kap = kbuf[:, kslot * 512 + sub * 128:
                                       kslot * 512 + (sub + 1) * 128]
                            p.emit("tensor", lambda t, b=sap, k=kap, q=qap: t.matmul(
                                b, k, q, start=True, stop=False))
                            lap = rk4l[:,
                                       (h * 4 + sub) * 128:(h * 4 + sub + 1) * 128]
                            rap = rkb[:,
                                      rslot * 16384 + blk * 512:
                                      rslot * 16384 + (blk + 1) * 512]
                            p.emit("tensor", lambda t, b=sap, l=lap, r=rap: t.matmul(
                                b, l, r, start=False, stop=True), inc="qk")
                        nqk = p.sem_n["qk"]
                        if first:
                            p.wait("vector", "qk", nqk)
                            p.wait("vector", "tbl2", ntbl1)
                            p.wait("vector", "av", p.sem_n["av"])
                            p.emit("vector", lambda v, b=scap, half2=half2:
                                   v.tensor_tensor(
                                       b, b, tri[:, half2 * 1024:(half2 + 1) * 1024],
                                       AluOpType.add), inc="av")
                        pslot = ptb_i["i"] % 4
                        ptb_i["i"] += 1
                        sm, vv = ptfree[pslot]
                        p.wait("scalar", sm, vv)
                        p.wait("scalar", "qk", nqk)
                        if first:
                            p.wait("scalar", "av", p.sem_n["av"])
                        p.emit("scalar", lambda sc, b=scap, ps=pslot: sc.activation(
                            ptb[:, ps * 1024:(ps + 1) * 1024], b, AF.Exp),
                            inc="ex")
                        bank_release(scn0, "ex")
                        bank_release(scn1, "ex")
                        nex = p.sem_n["ex"]

                        def pv_emit(obank=obank, lbank=lbank, kslot=kslot,
                                    half2=half2, pslot=pslot, first=first,
                                    last=last, nex=nex, vv_val=vv_val,
                                    owait=(ofree[ob] if (first and half2 == 0)
                                           else None),
                                    lwait=(lfree[0] if (first and half2 == 0)
                                           else None),
                                    rslot=rslot, nqk=nqk):
                            p.wait("tensor", "ex", nex)
                            p.wait("tensor", vv_val[0], vv_val[1])
                            if owait is not None:
                                p.wait("tensor", owait[0], owait[1])
                                p.wait("tensor", lwait[0], lwait[1])
                            for s2 in range(2):
                                sub = 2 * half2 + s2
                                st = first and sub == 0
                                sp = last and sub == 3
                                vap = vbuf[:, kslot * 512 + sub * 128:
                                           kslot * 512 + (sub + 1) * 128]
                                pap = ptb[:, pslot * 1024 + s2 * 512:
                                          pslot * 1024 + (s2 + 1) * 512]
                                p.emit("tensor", lambda t, o=obank, v=vap, pp=pap,
                                       st=st, sp=sp:
                                       t.matmul(o[:, :], v, pp, start=st, stop=sp))
                                p.emit("tensor", lambda t, lb=lbank, pp=pap,
                                       st=st, sp=sp:
                                       t.matmul(lb[:, :], e0mat[:, :], pp,
                                                start=st, stop=sp), inc="pv")
                            ptfree[pslot] = ("pv", p.sem_n["pv"])
                            if half2 == 1:
                                vfree[kslot] = ("pv", p.sem_n["pv"])

                        pvq.append(pv_emit)
                        if len(pvq) > PVDEPTH:
                            pvq.pop(0)()
                        if half2 == 1:
                            kfree[kslot] = ("qk", p.sem_n["qk"])
                            rkfree[rslot] = ("qk", p.sem_n["qk"])

                while pvq:
                    pvq.pop(0)()
                lc = 0
                p.wait("vector", "pv", p.sem_n["pv"])
                p.wait("vector", "bcm", lbuf_bc[0])
                p.wait("vector", "lcv", p.sem_n["lcv"])
                p.emit("vector", lambda v, lb=lbank, lc=lc: v.tensor_copy(
                    lbuf[0:1, lc:lc + 512], lb[0:1, :]), inc="lcv")
                p.wait("vector", "lcv", p.sem_n["lcv"])
                p.emit("vector", lambda v, lc=lc: v.reciprocal(
                    lbuf[0:1, lc + 512:lc + 1024], lbuf[0:1, lc:lc + 512]),
                    inc="lcv")
                p.wait("tensor", "lcv", p.sem_n["lcv"])
                p.emit("tensor", lambda t, lb=lbank, lc=lc: t.matmul(
                    lb[:, :], onecol32[0:1, :], lbuf[0:1, lc + 512:lc + 1024],
                    start=True, stop=True), inc="bcm")
                lbuf_bc[0] = p.sem_n["bcm"]
                p.wait("vector", "bcm", p.sem_n["bcm"])
                p.wait("vector", "oev", lbc_last[ob])
                p.emit("vector", lambda v, lb=lbank, ob=ob: v.tensor_copy(
                    lbc[:, ob * 512:(ob + 1) * 512], lb[:, :]), inc="lcv")
                lfree[0] = ("lcv", p.sem_n["lcv"])
                p.wait("vector", "lcv", p.sem_n["lcv"])
                p.emit("vector", lambda v, o=obank, h=h, s=s, ob=ob: v.tensor_tensor(
                    y_fm[:, h, s * 512:(s + 1) * 512], o[:, :],
                    lbc[:, ob * 512:(ob + 1) * 512], AluOpType.mult), inc="oev")
                ofree[ob] = ("oev", p.sem_n["oev"])
                lbc_last[ob] = p.sem_n["oev"]
        n_attn = p.sem_n["oev"]
        es2a.close()
        h2_fm = es.enter_context(nc.sbuf_tensor("h2_fm", [128, KC, TOW], BF16, side="right"))

        if dbg >= 1:
            for kc in range(KC):
                p.wait("vector", "oev", n_attn)
                p.wait("vector", "dbgw", p.sem_n["dbgw"])
                p.emit("vector", lambda v, kc=kc: v.tensor_copy(
                    dbst[:, :], y_fm[:, kc, :]), inc="v0")
                p.wait("sync", "v0", p.sem_n["v0"])
                p.dma("sync", dbg_y[128 * kc:128 * (kc + 1), :], dbst[:, :], "dbgw")

        es2b = ExitStack()
        sb2b = lambda n, s, d: es2b.enter_context(nc.sbuf_tensor(n, s, d))
        wo_tmp = sb2b("wo_tmp", [128, KC, TOW], BF16)
        xre = sb2b("xre", [128, 2 * 2048], F32)
        x1t = sb2b("x1t", [128, 2 * 2048], F32)
        z2_sb = sb2b("z2_sb", [128, 2 * 2048], F32)

        for fo in range(16):
            def evac(tt, bnk, bname, gv, fo=fo):
                sem, val = gv
                p.wait("vector", sem, val)
                p.wait("vector", "v0", nmod_v)
                p.emit("vector", lambda v, b=bnk, fo=fo, tt=tt: v.tensor_scalar(
                    wo_tmp[:, fo, tt * 512:(tt + 1) * 512], b[:, :],
                    modv[:, 32 + fo:33 + fo], None, AluOpType.mult), inc="gev")
                bank_release(bname, "gev")

            bsel = [rot[(2 * fo + t) % 4] for t in range(2)]
            proj_group(wo_d[fo, :, :], 2048, KC, 128,
                       lambda k, tt: y_fm[:, k, tt * 512:(tt + 1) * 512],
                       bsel, evac, rhs_waits=[("oev", n_attn)])
        n_wo = p.sem_n["gev"]

        xre_free = [("lnv", 0), ("lnv", 0)]
        x1t_free = [[], []]
        z2free = [("lnt", 0), ("lnt", 0)]
        for i in range(8):
            ii = i % 2
            sm, vv = xre_free[ii]
            p.wait("sync", sm, vv)
            xsm2 = f"xr{ii}"
            p.dma("sync", xre[:, ii * 2048:(ii + 1) * 2048],
                  xin[i * 128:(i + 1) * 128, :], xsm2)
            nxr = (xsm2, p.sem_n[xsm2])
            for half in range(2):
                p.wait("tensor", "gev", n_wo)
                bank_wait("tensor", "pT")
                for jj in range(8):
                    kc = half * 8 + jj
                    p.emit("tensor", lambda t, j=jj, kc=kc, i=i: t.matmul(
                        pT[:, 128 * j:128 * (j + 1)],
                        wo_tmp[:, kc, 128 * i:128 * (i + 1)], iden16[:, :],
                        start=(j == 0), stop=(j == 7), is_transpose=True),
                        inc=("lnt" if jj == 7 else None))
                p.wait("vector", "lnt", p.sem_n["lnt"])
                p.wait("vector", nxr[0], nxr[1])
                for (sem, val) in x1t_free[ii]:
                    p.wait("vector", sem, val)
                p.emit("vector", lambda v, half=half, ii=ii: v.tensor_tensor(
                    x1t[:, ii * 2048 + half * 1024:ii * 2048 + (half + 1) * 1024],
                    pT[:, :],
                    xre[:, ii * 2048 + half * 1024:ii * 2048 + (half + 1) * 1024],
                    AluOpType.add), inc="lnv")
                bank_release("pT", "lnv")
            nadd = p.sem_n["lnv"]
            xre_free[ii] = ("lnv", nadd)
            xsem = f"x1w{ii}"
            p.wait("sync", "lnv", nadd)
            p.dma("sync", x1d.ap()[i * 128:(i + 1) * 128, :],
                  x1t[:, ii * 2048:(ii + 1) * 2048], xsem)
            if dbg >= 1:
                p.dma("sync", dbg_x1[i * 128:(i + 1) * 128, :],
                      x1t[:, ii * 2048:(ii + 1) * 2048], "dbgw")
            emit_ln(x1t[:, ii * 2048:(ii + 1) * 2048],
                    z2_sb[:, ii * 2048:(ii + 1) * 2048],
                    lambda kc, i=i: h2_fm[:, kc, 128 * i:128 * (i + 1)],
                    4, 3, [("lnv", nadd)], [z2free[ii]],
                    [("oev", n_attn)], par=ii)
            if i == 3:
                lne_half = p.sem_n["lne"]
            z2free[ii] = ("lnt", p.sem_n["lnt"])
            x1t_free[ii] = [(xsem, p.sem_n[xsem]), ("lns", p.sem_n["lns"])]
            if dbg >= 1:
                x1t_free[ii].append(("dbgw", p.sem_n["dbgw"]))
        n_h2 = p.sem_n["lne"]
        es2b.close()
        es2.close()

        if dbg >= 2:
            for kc in range(KC):
                p.wait("vector", "lne", n_h2)
                p.wait("vector", "dbgw", p.sem_n["dbgw"])
                p.emit("vector", lambda v, kc=kc: v.tensor_copy(
                    dbst[:, :], h2_fm[:, kc, :]), inc="v0")
                p.wait("sync", "v0", p.sem_n["v0"])
                p.dma("sync", dbg_h2[128 * kc:128 * (kc + 1), :], dbst[:, :], "dbgw")

        es3 = ExitStack()
        sb3 = lambda n, s, d: es3.enter_context(nc.sbuf_tensor(n, s, d))
        bcw = sb3("bcw", [128, 9 * 1024], BF16)
        es3a = ExitStack()
        sb3a = lambda n, s, d: es3a.enter_context(nc.sbuf_tensor(n, s, d))
        g_sb = sb3a("g_sb", [9, TOW], F32)
        gtile = sb3a("gtile", [128, 8, 9], F32)
        gw = sb3a("gw", [128, 8, 8], F32)
        gpr = sb3a("gpr", [128, 8, 8], F32)
        gpn = sb3a("gpn", [128, 8, 8], F32)
        wrow = sb3a("wrow", [1, 9 * TOW], F32)

        p.wait("vector", "lnv", p.sem_n["lnv"])

        def evac_g(tt, bnk, bname, gv):
            sem, val = gv
            p.wait("vector", sem, val)
            p.wait("vector", "gtv", p.sem_n["gtv"])
            p.emit("vector", lambda v, b=bnk, tt=tt: v.tensor_copy(
                g_sb[0:9, tt * 512:(tt + 1) * 512], b), inc="gtv")
            bank_release(bname, "gtv")

        proj_group(g9_d[:, :], 144, KC, 9,
                   lambda k, tt: h2_fm[:, k, tt * 512:(tt + 1) * 512],
                   [rot9[0], rot9[1]],
                   evac_g, rhs_waits_tt=([("lne", lne_half)], [("lne", n_h2)]))

        for half in range(2):
            p.wait("tensor", "gtv", p.sem_n["gtv"])
            bank_wait("tensor", "pb2")
            for jj in range(4):
                i = 4 * half + jj
                p.emit("tensor", lambda t, i=i, jj=jj: t.matmul(
                    pb2[:, 9 * jj:9 * (jj + 1)], g_sb[0:9, 128 * i:128 * (i + 1)],
                    iden32[0:9, 0:9], start=(jj == 0), stop=(jj == 3),
                    is_transpose=True), inc=("gmm" if jj == 3 else None))
            p.wait("vector", "gmm", p.sem_n["gmm"])
            p.wait("vector", "gtv", p.sem_n["gtv"])
            p.emit("vector", lambda v, half=half: v.tensor_copy(
                gtile[:, 4 * half:4 * (half + 1), :], pb2[:, 0:36]), inc="gtv")
            bank_release("pb2", "gtv")
        p.wait("scalar", "gtv", p.sem_n["gtv"])
        p.wait("scalar", "gts", p.sem_n["gts"])
        p.emit("scalar", lambda s: s.activation(
            gw[:, :, :], gtile[:, :, 0:8], AF.Exp), inc="gts")
        p.wait("vector", "gts", p.sem_n["gts"])
        p.emit("vector", lambda v: v.reduce_sum(
            bnbuf[:, 0:8], gw[:, :, :], axis=AX.X), inc="gtv")
        p.wait("vector", "gtv", p.sem_n["gtv"])
        p.emit("vector", lambda v: v.reciprocal(bnbuf[:, 8:16], bnbuf[:, 0:8]),
               inc="gtv")
        p.wait("vector", "gtv", p.sem_n["gtv"])
        rcp_b = None
        p.emit("vector", lambda v: v.tensor_tensor(
            gpr[:, :, :], gw[:, :, :],
            bnbuf[:, 8:16].rearrange("p (a b) -> p a b", b=1).broadcast_to((128, 8, 8)),
            AluOpType.mult), inc="gtv")
        p.wait("vector", "gtv", p.sem_n["gtv"])
        p.emit("vector", lambda v: v.reduce_max(
            bnbuf[:, 16:24], gpr[:, :, :], axis=AX.X), inc="gtv")
        p.wait("vector", "gtv", p.sem_n["gtv"])
        p.emit("vector", lambda v: v.tensor_tensor(
            gpn[:, :, :], gpr[:, :, :],
            bnbuf[:, 16:24].rearrange("p (a b) -> p a b", b=1).broadcast_to((128, 8, 8)),
            AluOpType.is_ge), inc="gtv")
        p.wait("vector", "gtv", p.sem_n["gtv"])
        p.emit("vector", lambda v: v.tensor_scalar(
            gpn[:, :, :], gpn[:, :, :], NEG, None, AluOpType.mult), inc="gtv")
        p.wait("vector", "gtv", p.sem_n["gtv"])
        p.emit("vector", lambda v: v.tensor_tensor(
            gpn[:, :, :], gpn[:, :, :], gpr[:, :, :], AluOpType.add), inc="gtv")
        p.wait("vector", "gtv", p.sem_n["gtv"])
        p.emit("vector", lambda v: v.reduce_max(
            bnbuf[:, 24:32], gpn[:, :, :], axis=AX.X), inc="gtv")
        p.wait("vector", "gtv", p.sem_n["gtv"])
        p.emit("vector", lambda v: v.tensor_tensor(
            gpn[:, :, :], gpr[:, :, :],
            bnbuf[:, 24:32].rearrange("p (a b) -> p a b", b=1).broadcast_to((128, 8, 8)),
            AluOpType.is_ge), inc="gtv")
        p.wait("vector", "gtv", p.sem_n["gtv"])
        p.emit("vector", lambda v: v.tensor_tensor(
            gtile[:, :, 0:8], gpr[:, :, :], gpn[:, :, :], AluOpType.mult),
            inc="gtv")
        p.wait("scalar", "gtv", p.sem_n["gtv"])
        p.wait("scalar", "gts", p.sem_n["gts"])
        p.emit("scalar", lambda s: s.activation(
            gtile[:, :, 8:9], gtile[:, :, 8:9], AF.Sigmoid), inc="gts")
        for half in range(2):
            p.wait("tensor", "gtv", p.sem_n["gtv"])
            p.wait("tensor", "gts", p.sem_n["gts"])
            bank_wait("tensor", "pb2")
            for jj in range(4):
                i = 4 * half + jj
                p.emit("tensor", lambda t, i=i, jj=jj: t.matmul(
                    pb2[0:9, 128 * jj:128 * (jj + 1)], gtile[:, i, :],
                    iden32[:, :], start=(jj == 0), stop=(jj == 3),
                    is_transpose=True), inc=("gmm" if jj == 3 else None))
            p.wait("vector", "gmm", p.sem_n["gmm"])
            p.wait("vector", "gtv", p.sem_n["gtv"])
            p.emit("vector", lambda v, half=half: v.tensor_copy(
                g_sb[0:9, 512 * half:512 * (half + 1)], pb2[0:9, 0:512]),
                inc="gtv")
            bank_release("pb2", "gtv")
        p.wait("sync", "gtv", p.sem_n["gtv"])
        p.dma("sync", wd9.ap()[:, :], g_sb[0:9, :], "wd9w")
        p.wait("sync", "wd9w", p.sem_n["wd9w"])
        p.dma("sync", wrow[0:1, :], wd9.ap().rearrange("a b -> (a b)").rearrange("(a b) -> a b", a=1), "wrl")
        nwrow = p.sem_n["wrl"]
        if dbg >= 1:
            p.dma("sync", dbg_w[:, :], wd9.ap()[:, :], "dbgw")
        for r in range(9):
            for tt in range(2):
                p.wait("tensor", "wrl", nwrow)
                bank_wait("tensor", "pb2")
                p.emit("tensor", lambda t, r=r, tt=tt: t.matmul(
                    pb2[:, :], onecol32[0:1, :],
                    wrow[0:1, r * 1024 + tt * 512:r * 1024 + (tt + 1) * 512],
                    start=True, stop=True), inc="gmm")
                p.wait("vector", "gmm", p.sem_n["gmm"])
                p.wait("vector", "gtv", p.sem_n["gtv"])
                p.emit("vector", lambda v, r=r, tt=tt: v.tensor_copy(
                    bcw[:, r * 1024 + tt * 512:r * 1024 + (tt + 1) * 512],
                    pb2[:, :]), inc="gtv")
                bank_release("pb2", "gtv")
        n_bcw = p.sem_n["gtv"]
        es3a.close()

        es3b = ExitStack()
        sb3b = lambda n, s, d: es3b.enter_context(nc.sbuf_tensor(n, s, d))
        comb = sb3b("comb", [128, KC * TOW], F32)
        es3b1 = ExitStack()
        sb3b1 = lambda n, s, d: es3b1.enter_context(nc.sbuf_tensor(n, s, d))
        A_p = [sb3b1(f"A{jj}", [128, 11, TOW], BF16) for jj in range(2)]
        gtmp = sb3b1("gtmp", [128, 1024], BF16)
        utmp = sb3b1("utmp", [128, 1024], BF16)

        p.wait("vector", "lne", n_h2)
        p.wait("vector", "cmb", p.sem_n["cmb"])
        p.wait("vector", "x1w0", p.sem_n["x1w0"])
        p.wait("vector", "x1w1", p.sem_n["x1w1"])
        if dbg >= 1:
            p.wait("vector", "dbgw", p.sem_n["dbgw"])
        for kc in range(KC):
            p.emit("vector", lambda v, kc=kc: v.memset(
                comb[:, kc * 1024:(kc + 1) * 1024], 0.0), inc="cmb")
        ncmb0 = p.sem_n["cmb"]

        pieces = [("s", q) for q in range(4)] + [("e", e) for e in range(8)]
        a_last_use = [("wc", 0), ("wc", 0)]

        def build_A(pi, kind, idx):
            Ab = A_p[pi % 2]
            wcol = 8 if kind == "s" else idx
            asm, asv = a_last_use[pi % 2]
            for jj in range(11):
                if kind == "s":
                    gsrc = sgu_d[11 * idx + jj, :, :]
                    usrc = sgu_d[44 + 11 * idx + jj, :, :]
                else:
                    gsrc = w13_d[idx, jj, :, :]
                    usrc = w13_d[idx, 11 + jj, :, :]

                def evg(tt, bnk, bname, gv):
                    sem, val = gv
                    p.wait("scalar", sem, val)
                    p.wait("scalar", "aev", p.sem_n["aev"])
                    p.wait("scalar", "asv", p.sem_n["asv"])
                    p.emit("scalar", lambda s, b=bnk, tt=tt: s.activation(
                        gtmp[:, tt * 512:(tt + 1) * 512], b[:, :], AF.Silu),
                        inc="asv")
                    bank_release(bname, "asv")

                def evu(tt, bnk, bname, gv, wcol=wcol):
                    sem, val = gv
                    p.wait("vector", sem, val)
                    p.wait("vector", "gtv", n_bcw)
                    p.wait("vector", "aev", p.sem_n["aev"])
                    p.wait("vector", "auv", p.sem_n["auv"])
                    p.emit("vector", lambda v, b=bnk, tt=tt, w=wcol: v.tensor_tensor(
                        utmp[:, tt * 512:(tt + 1) * 512], b[:, :],
                        bcw[:, w * 1024 + tt * 512:w * 1024 + (tt + 1) * 512],
                        AluOpType.mult), inc="auv")
                    bank_release(bname, "auv")

                bsel0 = [rot[(2 * jj + t) % 4] for t in range(2)]
                bsel1 = [rot[(2 * jj + 2 + t) % 4] for t in range(2)]
                rw = [(asm, asv)] if jj == 0 else []
                rwtt = ([("lne", lne_half)], [("lne", n_h2)])
                proj_group(gsrc, 2048, KC, 128,
                           lambda k, tt: h2_fm[:, k, tt * 512:(tt + 1) * 512],
                           bsel0, evg, rhs_waits=rw, rhs_waits_tt=rwtt)
                proj_group(usrc, 2048, KC, 128,
                           lambda k, tt: h2_fm[:, k, tt * 512:(tt + 1) * 512],
                           bsel1, evu, rhs_waits_tt=rwtt)
                p.wait("vector", "asv", p.sem_n["asv"])
                p.wait("vector", "auv", p.sem_n["auv"])
                p.wait("vector", asm, asv)
                p.wait("vector", "aev", p.sem_n["aev"])
                p.emit("vector", lambda v, Ab=Ab, jj=jj: v.tensor_tensor(
                    Ab[:, jj, :], gtmp[:, :], utmp[:, :], AluOpType.mult),
                    inc="aev")

        def down_pair(pi0, pc0, pi1, pc1):
            nae = p.sem_n["aev"]
            for fo in range(16):
                for half, (pi, (kind, idx)) in enumerate([(pi0, pc0), (pi1, pc1)]):
                    src = sdn_d[idx, fo, :, :] if kind == "s" else w2_d[idx, fo, :, :]
                    gi, slot, wlv = wload(src, 1408)
                    p.wait("tensor", wlv[0], wlv[1])
                    if half == 0 and fo == 0:
                        p.wait("tensor", "aev", nae)
                    Ab = A_p[pi % 2]
                    for tt in range(2):
                        bnk, bname = obanks[tt], f"pb{tt}"
                        if half == 0:
                            bank_wait("tensor", bname)
                        for k in range(11):
                            wap = wring[:, slot * 2048 + k * 128:
                                        slot * 2048 + (k + 1) * 128]
                            rap = Ab[:, k, tt * 512:(tt + 1) * 512]
                            p.emit("tensor", lambda t, b=bnk, w=wap, r=rap,
                                   st=(half == 0 and k == 0),
                                   sp=(half == 1 and k == 10):
                                   t.matmul(b[:, :], w, r, start=st, stop=sp),
                                   inc=("wc" if (k == 10 and tt == 1) else None))
                    wdone(gi)
                nwc = p.sem_n["wc"]
                for tt in range(2):
                    bnk, bname = obanks[tt], f"pb{tt}"
                    p.wait("vector", "wc", nwc)
                    p.wait("vector", "cmb", ncmb0)
                    p.wait("vector", "dev", p.sem_n["dev"])
                    p.wait("vector", "v0", nmod_v)
                    cap = comb[:, fo * 1024 + tt * 512:fo * 1024 + (tt + 1) * 512]
                    p.emit("vector", lambda v, b=bnk, c=cap, fo=fo:
                           v.scalar_tensor_tensor(
                               c, b[:, :], modv[:, 80 + fo:81 + fo], c,
                               AluOpType.mult, AluOpType.add), inc="dev")
                    bank_release(bname, "dev")
            a_last_use[pi0 % 2] = ("wc", p.sem_n["wc"])
            a_last_use[pi1 % 2] = ("wc", p.sem_n["wc"])

        for pp in range(6):
            build_A(2 * pp, *pieces[2 * pp])
            build_A(2 * pp + 1, *pieces[2 * pp + 1])
            down_pair(2 * pp, pieces[2 * pp], 2 * pp + 1, pieces[2 * pp + 1])
        n_dev = p.sem_n["dev"]
        es3b1.close()

        es3c = ExitStack()
        sb3c = lambda n, s, d: es3c.enter_context(nc.sbuf_tensor(n, s, d))
        x1re = sb3c("x1re", [128, 2 * 2048], F32)
        outt = sb3c("outt", [128, 2 * 2048], F32)
        x1re_free = [("fin", 0), ("fin", 0)]
        n_x1w = [p.sem_n["x1w0"], p.sem_n["x1w1"]]
        for i in range(8):
            ii = i % 2
            sm, vv = x1re_free[ii]
            p.wait("sync", sm, vv)
            p.wait("sync", "x1w0", n_x1w[0])
            p.wait("sync", "x1w1", n_x1w[1])
            p.wait("sync", "dev", n_dev)
            qsm = f"x1r{ii}"
            p.dma("sync", x1re[:, ii * 2048:(ii + 1) * 2048],
                  x1d.ap()[i * 128:(i + 1) * 128, :], qsm)
            nx1r = (qsm, p.sem_n[qsm])
            osem = f"outw{ii}"
            for quad in range(4):
                bnk0, bname = rot[quad]
                bnk = bnk0
                p.wait("tensor", "dev", n_dev)
                bank_wait("tensor", bname)
                for jj in range(4):
                    kc = quad * 4 + jj
                    p.emit("tensor", lambda t, b=bnk, j=jj, kc=kc, i=i: t.matmul(
                        b[:, 128 * j:128 * (j + 1)],
                        comb[:, kc * 1024 + 128 * i:kc * 1024 + 128 * (i + 1)],
                        iden32[:, :], start=(j == 0), stop=(j == 3),
                        is_transpose=True), inc=("lnt" if jj == 3 else None))
                p.wait("vector", "lnt", p.sem_n["lnt"])
                p.wait("vector", nx1r[0], nx1r[1])
                p.wait("vector", osem, p.sem_n[osem])
                p.emit("vector", lambda v, b=bnk, quad=quad, ii=ii: v.tensor_tensor(
                    outt[:, ii * 2048 + quad * 512:ii * 2048 + (quad + 1) * 512],
                    b[:, :],
                    x1re[:, ii * 2048 + quad * 512:ii * 2048 + (quad + 1) * 512],
                    AluOpType.add), inc="fin")
                bank_release(bname, "fin")
            x1re_free[ii] = ("fin", p.sem_n["fin"])
            p.wait("sync", "fin", p.sem_n["fin"])
            p.dma("sync", yout[i * 128:(i + 1) * 128, :],
                  outt[:, ii * 2048:(ii + 1) * 2048], osem)
        es3c.close()
        es3b.close()
        es3.close()

        n_o0, n_o1 = p.sem_n["outw0"], p.sem_n["outw1"]
        ndbg = p.sem_n["dbgw"]
        p.items["sync"].append(lambda e, v=n_o0: e.wait_ge(p.handles["outw0"], v))
        p.items["sync"].append(lambda e, v=n_o1: e.wait_ge(p.handles["outw1"], v))
        if ndbg:
            p.items["sync"].append(
                lambda e, v=ndbg: e.wait_ge(p.handles["dbgw"], v))

        with nc.Block() as block:
            @block.tensor
            def _(t):
                for fn in p.items["tensor"]:
                    fn(t)

            @block.vector
            def _(v):
                for fn in p.items["vector"]:
                    fn(v)

            @block.scalar
            def _(s):
                for fn in p.items["scalar"]:
                    fn(s)

            @block.sync
            def _(sy):
                for fn in p.items["sync"]:
                    fn(sy)

            @block.gpsimd
            def _(g):
                for fn in p.items["gpsimd"]:
                    fn(g)

    return nc


def _prep_shared(inputs):
    W_qkv = np.asarray(inputs["L0_attn_W_qkv"], dtype=np.float32)
    W_o = np.asarray(inputs["L0_attn_W_o"], dtype=np.float32)
    gate_w = np.asarray(inputs["L0_ffn_gate_w"], dtype=np.float32)
    sgw = np.asarray(inputs["L0_ffn_shared_gate_w"], dtype=np.float32)
    sguw = np.asarray(inputs["L0_ffn_shared_gate_up_w"], dtype=np.float32)
    sdw = np.asarray(inputs["L0_ffn_shared_down_w"], dtype=np.float32)
    w13 = np.asarray(inputs["L0_ffn_w13"], dtype=np.float32)
    w2 = np.asarray(inputs["L0_ffn_w2"], dtype=np.float32)
    wmod = np.asarray(inputs["L0_wmod"], dtype=np.float32)
    c = np.asarray(inputs["L0_c"], dtype=np.float32)

    d = {}
    d["wq"] = _tile_w(W_qkv)
    d["wo"] = _tile_w(W_o)
    g9 = np.concatenate([gate_w.T, sgw.T], axis=1)
    d["g9"] = np.ascontiguousarray(
        g9.reshape(16, 128, 9).transpose(1, 0, 2).reshape(128, 144),
        dtype=np.float32)
    d["sgu"] = _tile_w(np.ascontiguousarray(sguw.T))
    sdT = np.ascontiguousarray(sdw.T)
    d["sdn"] = np.ascontiguousarray(np.stack(
        [_tile_w(sdT[1408 * q:1408 * (q + 1), :]) for q in range(4)]))
    d["w13"] = np.ascontiguousarray(np.stack(
        [_tile_w(np.ascontiguousarray(w13[e].T)) for e in range(8)]))
    d["w2"] = np.ascontiguousarray(np.stack(
        [_tile_w(np.ascontiguousarray(w2[e].T)) for e in range(8)]))
    d["cin"] = np.ascontiguousarray(c.reshape(D))
    wmod_slices = []
    for core in range(NCORES):
        ws = wmod[:, 1536 * core:1536 * (core + 1)]
        t = ws.reshape(16, 128, 3, 512).transpose(2, 1, 0, 3).reshape(3, 128, 8192)
        wmod_slices.append(np.ascontiguousarray(t, dtype=np.float32))
    d["wmod_slices"] = wmod_slices
    d["rk4l"] = _host_rk4l()
    d["tri"] = _host_tri()
    d["iden16"] = np.eye(128, dtype=BF)
    d["iden32"] = np.eye(128, dtype=np.float32)
    d["ones128"] = np.ones((128, 1), dtype=BF)
    e0 = np.zeros((128, 128), dtype=BF)
    e0[:, 0] = BF(1.0)
    d["e0mat"] = e0
    d["onecol32"] = np.ones((1, 128), dtype=np.float32)
    d["rk4r"] = [_host_rk4r(core) for core in range(NCORES)]
    return d


def _run(inputs, dbg=0):
    x = np.asarray(inputs["L0_attn_x"], dtype=np.float32).reshape(S, D)
    d = _prep_shared(inputs)
    nc = build_nc(dbg=dbg)
    in_maps = []
    for core in range(NCORES):
        xc = np.ascontiguousarray(
            np.concatenate([x[CH * core:CH * (core + 1)],
                            x[CH * (15 - core):CH * (16 - core)]], axis=0))
        in_maps.append(
            {"xin": xc, "cin": d["cin"], "wmods": d["wmod_slices"][core],
             "wq": d["wq"], "wo": d["wo"], "g9": d["g9"], "sgu": d["sgu"],
             "sdn": d["sdn"], "w13": d["w13"], "w2": d["w2"],
             "rk4l": d["rk4l"], "rk4r": d["rk4r"][core], "tri": d["tri"],
             "iden16": d["iden16"], "iden32": d["iden32"],
             "ones128": d["ones128"], "onecol32": d["onecol32"],
             "e0mat": d["e0mat"]})
    res = run_bass_kernel_spmd(nc, in_maps, core_ids=list(range(NCORES)))
    out = np.empty((S, D), dtype=np.float32)
    for core in range(NCORES):
        yc = res.results[core]["yout"]
        out[CH * core:CH * (core + 1)] = yc[0:CH]
        out[CH * (15 - core):CH * (16 - core)] = yc[CH:2 * CH]
    return out.reshape(1, S, D), res


def kernel(**inputs):
    out, _ = _run(inputs, dbg=0)
    return out
